# version 15
# baseline (speedup 1.0000x reference)
import numpy as np
import concourse.bass as bass
import concourse.mybir as mybir
from concourse.bass_utils import run_bass_kernel_spmd

F32 = mybir.dt.float32
F16 = mybir.dt.float16
AF = mybir.ActivationFunctionType
ALU = mybir.AluOpType

NCORE = 8
S = 8192
D = 2048
TQ = 1024
DKV = 512
HKV = 8
HD = 64
G16 = 16
NTC = 16
TCW = 512
KCN = 64
EPS = 1e-5

QS = 0
OS = 32768
XH0, XH1, XH2 = 65536, 81920, 98304
XSQ = 114688
KSB0, KSB1 = 163840, 164864
VSTO = 8192
RBCK = 169984
VR0, VR1 = 0, 2048
MR0, MR1 = 4096, 6144
MQ32O, VQ32O = 65536, 69632
KB0, KB1 = 65536, 81920
VB0, VB1 = 98304, 106624
PB0, PB1, PB2, PB3 = 114944, 116992, 119040, 121088
XG0 = 123136
RDO = 163840
OSHIFT = 165888
DROW = 166912
SHFO = 212416
ONESLO = 123136
XBO = 65536
WKHO, WVHO = 131072, 147456
MBCQO, RBCQO = 131072, 135168
WOH0, WOH1 = 131072, 135168
WFH0, WFH1 = 139264, 143360
RRH0, RRH1 = 147456, 151552
TT1O, TT2O = 155648, 159744
WQH0, WQH1 = 163840, 167936
HIROWO, LOROWO = 163840, 167936
OUTS0, OUTS1 = 163840, 167936
MROWO = 172032
HIBCO, LOBCO = 172032, 176128
G1BO, G2BO = 180224, 182272
RH3O, RSQ3O = 184320, 186368
RSTDROWO = 188416
M3O, V3O, SD3O = 188416, 192512, 196608
ONES64O = 204800
ONESCO = 204928
ONESRHO = 204992
ONESRFO = 205248
MASKQO = 205760
MASKQ4O = 205888
RSTDCOLO = 206016
WSKNO = 206272
WSVNO = 207296
RS320, RS321 = 208320, 210368
BCH1O, BCH2O = 184320, 186368


class Plan:
    def __init__(self):
        self.steps = []
        self.cnt = {}

    def v(self, sem):
        return self.cnt.get(sem, 0)

    def step(self, eng, fn, waits=(), incs=(), attach=True):
        self.steps.append(
            (eng, fn, [(s, x) for (s, x) in waits if x > 0], list(incs), attach))
        for s, amt in incs:
            self.cnt[s] = self.cnt.get(s, 0) + amt


def build(dbg=()):
    nc = bass.Bass("TRN2", target_bir_lowering=False, debug=False)

    xq = nc.dram_tensor("xq", [D, TQ], F32, kind="ExternalInput").ap()
    xf = nc.dram_tensor("xf", [D, S], F32, kind="ExternalInput").ap()
    wq = nc.dram_tensor("wq", [D, D], F32, kind="ExternalInput").ap()
    wk = nc.dram_tensor("wk", [D, DKV], F32, kind="ExternalInput").ap()
    wv = nc.dram_tensor("wv", [D, DKV], F32, kind="ExternalInput").ap()
    wo = nc.dram_tensor("wo", [D, D], F32, kind="ExternalInput").ap()
    wft = nc.dram_tensor("wft", [D, D], F32, kind="ExternalInput").ap()
    rrt = nc.dram_tensor("rrt", [D, D], F32, kind="ExternalInput").ap()
    mq = nc.dram_tensor("mq", [128, 16], F32, kind="ExternalInput").ap()
    shf = nc.dram_tensor("shf", [64, 128], F32, kind="ExternalInput").ap()
    out = nc.dram_tensor("out", [D, TQ], F32, kind="ExternalOutput").ap()
    kfull = nc.dram_tensor("kfull", [HKV, HD, KCN, 128], F16, kind="Internal").ap()
    vfull = nc.dram_tensor("vfull", [HKV, 128, KCN, HD + 1], F16, kind="Internal").ap()

    dbg_out = {}
    for name, shape, dt in dbg:
        dbg_out[name] = nc.dram_tensor(name, shape, dt, kind="ExternalOutput").ap()

    def dbg_dump(name, src_ap, waits):
        if name in dbg_out:
            def f_dump(sy, name=name, src_ap=src_ap):
                return sy.dma_start(out=dbg_out[name][:, :], in_=src_ap)
            pl.step("sync", f_dump, waits=waits, incs=[("s_st", 16)])

    SBASE = 16640

    def sb(name, off, shape, dt):
        return nc.alloc_sbuf_tensor_at(name, shape, dt, offset=off + SBASE).ap()

    qsb = sb("qsb", QS, [128, 16384], F16)
    tqsb = sb("tqsb", QS, [128, 16384], F16)
    xqh = sb("xqh", OS, [128, 16384], F16)
    osb = sb("osb", OS, [128, 16384], F16)
    xh = [sb("xh0", XH0, [128, 8192], F16), sb("xh1", XH1, [128, 8192], F16),
          sb("xh2", XH2, [128, 8192], F16)]
    xsq = sb("xsq", XSQ, [128, 8192], F16)
    ksb = [sb("ksb0", KSB0, [128, 512], F16), sb("ksb1", KSB1, [128, 512], F16)]
    vst_full = sb("vst", VSTO, [128, 2144], F16)
    vst = vst_full[:, 0:2080]
    vrow = [sb("vr0", VR0, [1, 512], F32), sb("vr1", VR1, [1, 512], F32)]
    mrow32 = [sb("mr0", MR0, [1, 512], F32), sb("mr1", MR1, [1, 512], F32)]
    rbck = sb("rbck", RBCK, [128, 512], F32)
    mq32 = sb("mq32", MQ32O, [1, 1024], F32)
    vq32 = sb("vq32", VQ32O, [1, 1024], F32)
    kb = [sb("kb0", KB0, [128, 8192], F16), sb("kb1", KB1, [128, 8192], F16)]
    vb = [sb("vb0", VB0, [128, 4160], F16), sb("vb1", VB1, [128, 4160], F16)]
    pb = [sb("pb0", PB0, [128, 1024], F16), sb("pb1", PB1, [128, 1024], F16),
          sb("pb2", PB2, [128, 1024], F16), sb("pb3", PB3, [128, 1024], F16)]
    rdb = sb("rdb", RDO, [128, 512], F32)
    oshift = sb("oshift", OSHIFT, [64, 512], F16)
    drow = sb("drow", DROW, [1, 1024], F16)
    shfh = sb("shfh", SHFO, [64, 128], F16)
    oneslo = sb("oneslo", ONESLO, [1, 128], F16)
    oneshi = sb("oneshi", ONESLO + 256, [1, 128], F16)
    xb = sb("xb", XBO, [128, 16384], F32)
    wkh = sb("wkh", WKHO, [128, 8192], F16)
    wvh = sb("wvh", WVHO, [128, 8192], F16)
    mbcq = sb("mbcq", MBCQO, [128, 1024], F32)
    rbcq = sb("rbcq", RBCQO, [128, 1024], F32)
    woh = [sb("woh0", WOH0, [128, 2048], F16), sb("woh1", WOH1, [128, 2048], F16)]
    wfh = [sb("wfh0", WFH0, [128, 2048], F16), sb("wfh1", WFH1, [128, 2048], F16)]
    rrh = [sb("rrh0", RRH0, [128, 2048], F16), sb("rrh1", RRH1, [128, 2048], F16)]
    tt1 = sb("tt1", TT1O, [128, 2048], F16)
    tt2 = sb("tt2", TT2O, [128, 2048], F16)
    wqh = [sb("wqh0", WQH0, [128, 2048], F16), sb("wqh1", WQH1, [128, 2048], F16)]
    hirow = sb("hirow", HIROWO, [1, 1024], F32)
    lorow = sb("lorow", LOROWO, [1, 1024], F32)
    outs = [sb("outs0", OUTS0, [128, 1024], F32), sb("outs1", OUTS1, [128, 1024], F32)]
    mrowh = sb("mrowh", MROWO, [1, 8192], F16)
    hibc = sb("hibc", HIBCO, [128, 1024], F32)
    lobc = sb("lobc", LOBCO, [128, 1024], F32)
    g1b = sb("g1b", G1BO, [128, 1024], F16)
    g2b = sb("g2b", G2BO, [128, 1024], F16)
    rh3 = sb("rh3", RH3O, [128, 1024], F16)
    rsq3 = sb("rsq3", RSQ3O, [128, 1024], F16)
    rstdrowh = sb("rstdrowh", RSTDROWO, [1, 8192], F16)
    m3 = sb("m3", M3O, [1, 1024], F32)
    v3 = sb("v3", V3O, [1, 1024], F32)
    sd3 = sb("sd3", SD3O, [1, 1024], F32)
    ones64 = sb("ones64", ONES64O, [128, 64], F16)
    onesc = sb("onesc", ONESCO, [128, 1], F16)
    onesrh = sb("onesrh", ONESRHO, [1, 128], F16)
    onesrf = sb("onesrf", ONESRFO, [1, 128], F32)
    maskq = sb("maskq_sb", MASKQO, [128, 16], F32)
    maskq4 = sb("maskq4", MASKQ4O, [128, 64], F16)
    rstdcol = sb("rstdcol", RSTDCOLO, [128, 64], F32)
    wskn = sb("wskn", WSKNO, [1, 512], F16)
    wsvn = sb("wsvn", WSVNO, [1, 512], F16)
    rs32 = [sb("rs320", RS320, [1, 512], F32), sb("rs321", RS321, [1, 512], F32)]
    bch = sb("bch", BCH1O, [1, 1024], F16)
    bcl = sb("bcl", BCH2O, [1, 1024], F16)

    psA = nc.alloc_psum_tensor("psA", [128, 2, 512], F32).ap()
    psB = nc.alloc_psum_tensor("psB", [128, 2, 512], F32).ap()
    psC = nc.alloc_psum_tensor("psC", [128, 512], F32).ap()
    psD = nc.alloc_psum_tensor("psD", [128, 512], F32).ap()
    psE = nc.alloc_psum_tensor("psE", [128, 512], F32).ap()
    psF = nc.alloc_psum_tensor("psF", [128, 512], F32).ap()

    semnames = ["s_x0", "s_x1", "s_x2", "s_xq", "s_ms", "s_wk", "s_q0", "s_q1",
                "s_kb0", "s_kb1", "s_vb0", "s_vb1", "s_xb",
                "s_wo0", "s_wo1", "s_wf0", "s_wf1", "s_kvst", "s_st",
                "s_pe", "s_ac", "s_dv", "s_po"]
    sems = {}
    sem_ctxs = []
    for n in semnames:
        ctx = nc.semaphore(n)
        sems[n] = ctx.__enter__()
        sem_ctxs.append(ctx)

    pl = Plan()
    V = pl.v
    MS = {}

    def f_const(g):
        g.memset(ones64[:, :], 1.0)
        g.memset(onesc[:, :], 1.0)
        g.memset(onesrh[:, :], 1.0)
        return g.memset(onesrf[:, :], 1.0)
    pl.step("gpsimd", f_const, incs=[("s_po", 1)])

    pl.step("gpsimd", lambda g: g.dma_start(out=maskq[:, :], in_=mq[:, :]),
            incs=[("s_ms", 16)])

    pl.step("gpsimd", lambda g: g.dma_start(out=shfh[:, :], in_=shf[:, :]),
            incs=[("s_ms", 16)])

    def f_vones(g):
        return g.memset(vst_full[:, :], 1.0)
    pl.step("gpsimd", f_vones, incs=[("s_po", 1)])

    def f_mask4(g):
        for j in range(4):
            i = g.tensor_copy(out=maskq4[:, j * 16:(j + 1) * 16], in_=maskq[:, :])
        return i
    pl.step("gpsimd", f_mask4, waits=[("s_ms", 16)], incs=[("s_po", 1)])

    def f_xqload(g):
        return g.dma_start(out=xqh[:, :].rearrange("p (g t) -> p g t", g=16),
                           in_=xq.rearrange("(g p) t -> p g t", p=128))
    pl.step("gpsimd", f_xqload, incs=[("s_xq", 16)])

    def f_wkload(g):
        g.dma_start(out=wkh[:, :].rearrange("p (g n) -> p g n", g=16),
                    in_=wk.rearrange("(g p) n -> p g n", p=128)
                    ).then_inc(sems["s_wk"], 16)
        return g.dma_start(out=wvh[:, :].rearrange("p (g n) -> p g n", g=16),
                           in_=wv.rearrange("(g p) n -> p g n", p=128)
                           ).then_inc(sems["s_wk"], 16)
    pl.step("gpsimd", f_wkload, incs=[("s_wk", 32)], attach=False)

    def f_wsum_mm(t):
        for dc in range(G16):
            t.matmul(psC[0:64, :], ones64[:, :], wkh[:, dc * 512:(dc + 1) * 512],
                     start=(dc == 0), stop=(dc == 15))
        for dc in range(G16):
            i = t.matmul(psD[0:64, :], ones64[:, :], wvh[:, dc * 512:(dc + 1) * 512],
                         start=(dc == 0), stop=(dc == 15))
        return i
    pl.step("tensor", f_wsum_mm, waits=[("s_wk", 32), ("s_po", 1)],
            incs=[("s_pe", 1)])

    def f_wsum_ev(v):
        v.tensor_scalar_mul(wskn[:, :], psC[0:1, :], -1.0)
        return v.tensor_scalar_mul(wsvn[:, :], psD[0:1, :], -1.0)
    pl.step("vector", f_wsum_ev, waits=[("s_pe", V("s_pe"))], incs=[("s_dv", 1)])
    MS["wsum_dv"] = V("s_dv")

    pe_proj_done = [0] * NTC

    def emit_stats(tc):
        sl = tc % 3
        sx = "s_x%d" % sl
        wts = [(sx, V(sx))]
        if tc >= 3:
            wts.append(("s_pe", pe_proj_done[tc - 3]))

        def f_xload(g, tc=tc, sl=sl):
            return g.dma_start(
                out=xh[sl][:, :].rearrange("p (g t) -> p g t", g=16),
                in_=xf[:, tc * TCW:(tc + 1) * TCW].rearrange("(g p) t -> p g t", p=128))
        pl.step("gpsimd", f_xload, waits=wts, incs=[(sx, 16)])

        wts = [(sx, V(sx))]
        if tc >= 1:
            wts.append(("s_pe", MS["stats_pe_%d" % (tc - 1)]))

        def f_sq(g, sl=sl):
            return g.tensor_mul(xsq[:, :], xh[sl][:, :], xh[sl][:, :])
        pl.step("gpsimd", f_sq, waits=wts, incs=[("s_po", 1)])

        wts = [("s_po", V("s_po"))]
        if tc >= 1:
            wts.append(("s_dv", MS["rows_dv_%d" % (tc - 1)]))

        def f_stats(t, sl=sl):
            for g in range(G16):
                t.matmul(psE[0:64, :], ones64[:, :], xh[sl][:, g * 512:(g + 1) * 512],
                         start=(g == 0), stop=(g == 15))
            for g in range(G16):
                i = t.matmul(psF[0:64, :], ones64[:, :], xsq[:, g * 512:(g + 1) * 512],
                             start=(g == 0), stop=(g == 15))
            return i
        pl.step("tensor", f_stats, waits=wts, incs=[("s_pe", 1)])
        MS["stats_pe_%d" % tc] = V("s_pe")

        rl = tc % 2

        def f_rows(v, tc=tc, rl=rl):
            v.tensor_scalar_mul(mrow32[rl][:, :], psE[0:1, :], 1.0 / D)
            v.tensor_scalar_mul(vrow[rl][:, :], psF[0:1, :], 1.0 / D)
            v.tensor_mul(psE[0:1, :], mrow32[rl][:, :], mrow32[rl][:, :])
            v.tensor_tensor(vrow[rl][:, :], vrow[rl][:, :], psE[0:1, :], ALU.subtract)
            v.tensor_scalar_add(vrow[rl][:, :], vrow[rl][:, :], EPS)
            return v.tensor_copy(out=mrowh[0:1, tc * TCW:(tc + 1) * TCW],
                                 in_=mrow32[rl][:, :])
        pl.step("vector", f_rows, waits=[("s_pe", V("s_pe"))], incs=[("s_dv", 1)])
        MS["rows_dv_%d" % tc] = V("s_dv")

        def f_rstd(s, tc=tc, rl=rl):
            s.activation(out=mrow32[rl][:, :], in_=vrow[rl][:, :], func=AF.Ln)
            s.activation(out=rs32[rl][:, :], in_=mrow32[rl][:, :], func=AF.Exp,
                         scale=-0.5)
            return s.activation(out=rstdrowh[0:1, tc * TCW:(tc + 1) * TCW],
                                in_=rs32[rl][:, :], func=AF.Copy)
        pl.step("scalar", f_rstd, waits=[("s_dv", V("s_dv"))], incs=[("s_ac", 1)])
        MS["rstd_ac_%d" % tc] = V("s_ac")

    def emit_proj(tc):
        sl = tc % 3
        rl = tc % 2
        wts = [("s_ac", MS["rstd_ac_%d" % tc])]
        if tc >= 1:
            wts.append(("s_dv", MS["tcol_dv_%d" % (tc - 1)]))

        def f_tcol(t, tc=tc, rl=rl):
            for ts in range(4):
                t.transpose(psD[:, ts:ts + 1],
                            rs32[rl][0:1, ts * 128:(ts + 1) * 128],
                            onesrf[0:1, 0:1])
            return t.matmul(psC[:, :], onesrh[:, :],
                            rstdrowh[0:1, tc * TCW:(tc + 1) * TCW],
                            start=True, stop=True)
        pl.step("tensor", f_tcol, waits=wts, incs=[("s_pe", 1)])

        def f_tcol_ev(v, tc=tc):
            v.tensor_copy(out=rstdcol[:, tc * 4:tc * 4 + 4], in_=psD[:, 0:4])
            return v.tensor_copy(out=rbck[:, :], in_=psC[:, :])
        pl.step("vector", f_tcol_ev,
                waits=[("s_pe", V("s_pe")), ("s_dv", MS["wsum_dv"])],
                incs=[("s_dv", 1)])
        MS["tcol_dv_%d" % tc] = V("s_dv")

        for b in range(4):
            pslot = psA[:, b % 2, :]

            def f_kmm(t, tc=tc, sl=sl, b=b, pslot=pslot):
                for dc in range(G16):
                    t.matmul(pslot,
                             wkh[:, dc * 512 + b * 128: dc * 512 + (b + 1) * 128],
                             xh[sl][:, dc * 512:(dc + 1) * 512],
                             start=(dc == 0), stop=False)
                return t.matmul(pslot, wskn[0:1, b * 128:(b + 1) * 128],
                                mrowh[0:1, tc * TCW:(tc + 1) * TCW],
                                start=False, stop=True)
            wts = []
            prev = ("kev_dv_%d_%d" % (tc, b - 2)) if b >= 2 else \
                   ("kev_dv_%d_%d" % (tc - 1, b + 2)) if tc >= 1 else None
            if prev:
                wts.append(("s_dv", MS[prev]))
            pl.step("tensor", f_kmm, waits=wts, incs=[("s_pe", 1)])
            pe_k = V("s_pe")

            def f_kev(v, b=b, pslot=pslot):
                return v.tensor_tensor(ksb[b % 2][:, :], pslot, rbck[:, :], ALU.mult)
            wts = [("s_pe", pe_k), ("s_dv", MS["tcol_dv_%d" % tc])]
            prevst = ("kst_%d_%d" % (tc, b - 2)) if b >= 2 else \
                     ("kst_%d_%d" % (tc - 1, b + 2)) if tc >= 1 else None
            if prevst:
                wts.append(("s_kvst", MS[prevst]))
            pl.step("vector", f_kev, waits=wts, incs=[("s_dv", 1)])
            MS["kev_dv_%d_%d" % (tc, b)] = V("s_dv")

            def f_kst(sy, tc=tc, b=b):
                for e in range(2):
                    i = sy.dma_start(
                        out=kfull[2 * b + e, :, tc * 4:(tc + 1) * 4, :],
                        in_=ksb[b % 2][64 * e:64 * e + 64, :].rearrange(
                            "p (c k) -> p c k", c=4)
                    ).then_inc(sems["s_kvst"], 16)
                return i
            pl.step("sync", f_kst, waits=[("s_dv", V("s_dv"))],
                    incs=[("s_kvst", 32)], attach=False)
            MS["kst_%d_%d" % (tc, b)] = V("s_kvst")

        for ts in (0, 1, 3, 2):
            pslot = psB[:, ts % 2, :]

            def f_vmm(t, tc=tc, sl=sl, ts=ts, pslot=pslot):
                for dc in range(G16):
                    t.matmul(pslot,
                             xh[sl][:, dc * 512 + ts * 128: dc * 512 + (ts + 1) * 128],
                             wvh[:, dc * 512:(dc + 1) * 512],
                             start=(dc == 0), stop=False)
                return t.matmul(
                    pslot,
                    mrowh[0:1, tc * TCW + ts * 128: tc * TCW + (ts + 1) * 128],
                    wsvn[0:1, :], start=False, stop=True)
            wts = []
            prev = ("vev_dv_%d_%d" % (tc, ts - 2)) if ts >= 2 else \
                   ("vev_dv_%d_%d" % (tc - 1, ts + 2)) if tc >= 1 else None
            if prev:
                wts.append(("s_dv", MS[prev]))
            pl.step("tensor", f_vmm, waits=wts, incs=[("s_pe", 1)])
            pe_v = V("s_pe")

            def f_vev(v, tc=tc, ts=ts, pslot=pslot):
                return v.tensor_scalar(
                    out=vst[:, :].rearrange("p (h c d) -> p h c d", h=8, d=65)[:, :, ts, 0:64],
                    in0=pslot.rearrange("p (h d) -> p h d", h=8),
                    scalar1=rstdcol[:, tc * 4 + ts: tc * 4 + ts + 1],
                    scalar2=None, op0=ALU.mult)
            wts = [("s_pe", pe_v), ("s_dv", MS["tcol_dv_%d" % tc])]
            if tc >= 1 and ts == 0:
                wts.append(("s_kvst", MS["vst_%d" % (tc - 1)]))
            pl.step("vector", f_vev, waits=wts, incs=[("s_dv", 1)])
            MS["vev_dv_%d_%d" % (tc, ts)] = V("s_dv")

        def f_vst(sy, tc=tc):
            vflat = vfull.rearrange("h p c d -> h p (c d)")
            for h in range(8):
                i = sy.dma_start(
                    out=vflat[h, :, tc * 260:(tc + 1) * 260],
                    in_=vst[:, h * 260:(h + 1) * 260]
                ).then_inc(sems["s_kvst"], 16)
            return i
        pl.step("sync", f_vst, waits=[("s_dv", V("s_dv")), ("s_po", 2)],
                incs=[("s_kvst", 16 * 8)], attach=False)
        MS["vst_%d" % tc] = V("s_kvst")
        pe_proj_done[tc] = V("s_pe")

    for tc in range(NTC + 1):
        if tc < NTC:
            emit_stats(tc)
        if tc >= 1:
            emit_proj(tc - 1)
    MS["p1_pe"] = V("s_pe")
    MS["p1_dv"] = V("s_dv")
    MS["p1_kvst"] = V("s_kvst")
    dbg_dump("vst_dbg", vst_full[:, :], [("s_dv", V("s_dv"))])
    if "v7_dbg" in dbg_out:
        pl.step("sync", lambda sy: sy.dma_start(
            out=dbg_out["v7_dbg"][:, :],
            in_=vfull[7, :, :, :].rearrange("p c d -> p (c d)")),
            waits=[("s_kvst", V("s_kvst"))], incs=[("s_st", 16)])
    dbg_dump("mr_dbg", mrowh[:, :], [("s_dv", MS["rows_dv_%d" % (NTC - 1)])])
    dbg_dump("rs_dbg", rstdrowh[:, :], [("s_ac", MS["rstd_ac_%d" % (NTC - 1)])])
    dbg_dump("rc_dbg", rstdcol[:, :], [("s_dv", MS["p1_dv"])])

    for g16 in range(G16):
        def f_qsq(g, g16=g16):
            return g.tensor_mul(xsq[:, 0:1024],
                                xqh[:, g16 * 1024:(g16 + 1) * 1024],
                                xqh[:, g16 * 1024:(g16 + 1) * 1024])
        wts = [("s_xq", 16)]
        if g16 == 0:
            wts.append(("s_pe", MS["stats_pe_%d" % (NTC - 1)]))
        else:
            wts.append(("s_pe", MS["qstat_pe_%d" % (g16 - 1)]))
        pl.step("gpsimd", f_qsq, waits=wts, incs=[("s_po", 1)])

        wts = [("s_po", V("s_po"))]
        if g16 == 0:
            wts.append(("s_dv", MS["p1_dv"]))

        def f_qstat(t, g16=g16):
            for t2 in range(2):
                pm = psE if t2 == 0 else psF
                t.matmul(pm[0:64, :], ones64[:, :],
                         xqh[:, g16 * 1024 + t2 * 512: g16 * 1024 + (t2 + 1) * 512],
                         start=(g16 == 0), stop=(g16 == 15))
            for t2 in range(2):
                pm = psC if t2 == 0 else psD
                i = t.matmul(pm[0:64, :], ones64[:, :], xsq[:, t2 * 512:(t2 + 1) * 512],
                             start=(g16 == 0), stop=(g16 == 15))
            return i
        pl.step("tensor", f_qstat, waits=wts, incs=[("s_pe", 1)])
        MS["qstat_pe_%d" % g16] = V("s_pe")

    def f_qrows(v):
        for t2 in range(2):
            pm = psE if t2 == 0 else psF
            pv2 = psC if t2 == 0 else psD
            v.tensor_scalar_mul(mq32[0:1, t2 * 512:(t2 + 1) * 512], pm[0:1, :], 1.0 / D)
            v.tensor_scalar_mul(vq32[0:1, t2 * 512:(t2 + 1) * 512], pv2[0:1, :], 1.0 / D)
            v.tensor_mul(pm[0:1, :], mq32[0:1, t2 * 512:(t2 + 1) * 512],
                         mq32[0:1, t2 * 512:(t2 + 1) * 512])
            i = v.tensor_tensor(vq32[0:1, t2 * 512:(t2 + 1) * 512],
                                vq32[0:1, t2 * 512:(t2 + 1) * 512],
                                pm[0:1, :], ALU.subtract)
        return v.tensor_scalar_add(vq32[0:1, :], vq32[0:1, :], EPS)
    pl.step("vector", f_qrows,
            waits=[("s_pe", V("s_pe")), ("s_ac", MS["rstd_ac_%d" % (NTC - 1)])],
            incs=[("s_dv", 1)])
    MS["qrows_dv"] = V("s_dv")
    dbg_dump("mq_dbg", mq32[:, :], [("s_dv", MS["qrows_dv"])])

    def f_qrstd(s):
        s.activation(out=mbcq[0:1, 0:1024], in_=vq32[0:1, :], func=AF.Ln)
        return s.activation(out=vq32[0:1, :], in_=mbcq[0:1, 0:1024], func=AF.Exp,
                            scale=-0.5)
    pl.step("scalar", f_qrstd, waits=[("s_dv", V("s_dv"))], incs=[("s_ac", 1)])
    MS["qrstd_ac"] = V("s_ac")
    dbg_dump("rq_dbg", vq32[:, :], [("s_ac", MS["qrstd_ac"])])

    def f_qbc_prep1(v):
        v.tensor_copy(out=bch[:, :], in_=mq32[0:1, :])
        return v.tensor_tensor(bcl[:, :], mq32[0:1, :], bch[:, :], ALU.subtract)
    pl.step("vector", f_qbc_prep1, waits=[("s_ac", V("s_ac"))], incs=[("s_dv", 1)])

    def f_qbc1(t):
        for t2 in range(2):
            t.matmul(psA[:, t2, :], onesrh[:, :], bch[0:1, t2 * 512:(t2 + 1) * 512],
                     start=True, stop=False)
            i = t.matmul(psA[:, t2, :], onesrh[:, :], bcl[0:1, t2 * 512:(t2 + 1) * 512],
                         start=False, stop=True)
        return i
    pl.step("tensor", f_qbc1, waits=[("s_dv", V("s_dv")), ("s_dv", MS["p1_dv"])],
            incs=[("s_pe", 1)])

    def f_qbc_prep2(v):
        v.tensor_copy(out=bch[:, :], in_=vq32[0:1, :])
        return v.tensor_tensor(bcl[:, :], vq32[0:1, :], bch[:, :], ALU.subtract)
    pl.step("vector", f_qbc_prep2, waits=[("s_pe", V("s_pe"))], incs=[("s_dv", 1)])

    def f_qbc2(t):
        for t2 in range(2):
            t.matmul(psB[:, t2, :], onesrh[:, :], bch[0:1, t2 * 512:(t2 + 1) * 512],
                     start=True, stop=False)
            i = t.matmul(psB[:, t2, :], onesrh[:, :], bcl[0:1, t2 * 512:(t2 + 1) * 512],
                         start=False, stop=True)
        return i
    pl.step("tensor", f_qbc2, waits=[("s_dv", V("s_dv"))], incs=[("s_pe", 1)])

    def f_qbc_ev(v):
        v.tensor_copy(out=mbcq[:, :], in_=psA[:, :, :].rearrange("p a b -> p (a b)"))
        return v.tensor_copy(out=rbcq[:, :],
                             in_=psB[:, :, :].rearrange("p a b -> p (a b)"))
    pl.step("vector", f_qbc_ev, waits=[("s_pe", V("s_pe"))], incs=[("s_dv", 1)])
    MS["qbc_dv"] = V("s_dv")

    def f_ln1q(v):
        for g16 in range(G16):
            cs = g16 * 1024
            v.tensor_tensor(xqh[:, cs:cs + 1024], xqh[:, cs:cs + 1024], mbcq[:, :],
                            ALU.subtract)
            i = v.tensor_tensor(xqh[:, cs:cs + 1024], xqh[:, cs:cs + 1024], rbcq[:, :],
                                ALU.mult)
        return i
    pl.step("vector", f_ln1q, waits=[("s_dv", V("s_dv"))], incs=[("s_dv", 1)])
    MS["ln1q_dv"] = V("s_dv")
    dbg_dump("mbc_dbg", mbcq[:, :], [("s_dv", MS["qbc_dv"])])
    dbg_dump("rbc_dbg", rbcq[:, :], [("s_dv", MS["qbc_dv"])])
    dbg_dump("ln1q_dbg", xqh[:, :], [("s_dv", MS["ln1q_dv"])])

    for g16 in range(G16):
        sl = g16 % 2
        sq = "s_q%d" % sl

        def f_wqld(g, g16=g16, sl=sl):
            return g.dma_start(
                out=wqh[sl][:, :].rearrange("p (g n) -> p g n", g=16),
                in_=wq[:, g16 * 128:(g16 + 1) * 128].rearrange("(g p) n -> p g n", p=128))
        wts = [(sq, V(sq))]
        if g16 >= 2:
            wts.append(("s_pe", MS["qproj_pe_%d" % (g16 - 2)]))
        else:
            wts.append(("s_kvst", MS["p1_kvst"]))
        pl.step("gpsimd", f_wqld, waits=wts, incs=[(sq, 16)])

        def f_qproj(t, g16=g16, sl=sl):
            pm = psA if g16 % 2 == 0 else psB
            for t2 in range(2):
                for dc in range(G16):
                    i = t.matmul(pm[:, t2, :], wqh[sl][:, dc * 128:(dc + 1) * 128],
                                 xqh[:, dc * 1024 + t2 * 512: dc * 1024 + (t2 + 1) * 512],
                                 start=(dc == 0), stop=(dc == 15))
            return i
        wts = [(sq, V(sq)), ("s_dv", MS["ln1q_dv"])]
        if g16 >= 2:
            wts.append(("s_dv", MS["qev_dv_%d" % (g16 - 2)]))
        pl.step("tensor", f_qproj, waits=wts, incs=[("s_pe", 1)])
        MS["qproj_pe_%d" % g16] = V("s_pe")

        def f_qev(v, g16=g16):
            pm = psA if g16 % 2 == 0 else psB
            return v.tensor_copy(out=qsb[:, g16 * 1024:(g16 + 1) * 1024],
                                 in_=pm[:, :, :].rearrange("p a b -> p (a b)"))
        pl.step("vector", f_qev, waits=[("s_pe", V("s_pe"))], incs=[("s_dv", 1)])
        MS["qev_dv_%d" % g16] = V("s_dv")
    MS["p1c_pe"] = V("s_pe")
    MS["p1c_dv"] = V("s_dv")
    dbg_dump("q_dbg", qsb[:, :], [("s_dv", MS["p1c_dv"])])

    MS["exps_prev"] = [0, 0]
    MS["oev_guard"] = [0, 0]
    MS["bcsh_guard"] = 0

    passes = []
    for h in range(HKV):
        for qt in range(2):
            for pr in range(2):
                passes.append((h, qt, pr, 32 if qt == 0 else 64))
    sev = []
    for h, qt, pr, kcmax in passes:
        for kc in range(kcmax):
            sev.append((h, qt, pr, kc, kcmax))
    NEV = len(sev)
    kb_ready_h = {}
    vb_ready_h = {}
    exp_hist = {}
    mask_hist = {}

    def emit_kbld(h):
        sl = h % 2
        skb, svb = "s_kb%d" % sl, "s_vb%d" % sl

        def f_kbld(sy, h=h, sl=sl, skb=skb, svb=svb):
            sy.dma_start(out=kb[sl][0:64, :],
                         in_=kfull[h, :, :, :].rearrange("d c k -> d (c k)")
                         ).then_inc(sems[skb], 16)
            sy.dma_start(out=kb[sl][64:128, :],
                         in_=kfull[h, :, :, :].rearrange("d c k -> d (c k)")
                         ).then_inc(sems[skb], 16)
            return sy.dma_start(out=vb[sl][:, :],
                                in_=vfull[h, :, :, :].rearrange("p c d -> p (c d)")
                                ).then_inc(sems[svb], 16)
        wts = [("s_kvst", MS["p1_kvst"]), (skb, V(skb)), (svb, V(svb))]
        if h >= 2:
            wts.append(("s_pe", MS["attn_pe_h%d" % (h - 2)]))
        else:
            wts.append(("s_pe", MS["p1c_pe"]))
        pl.step("sync", f_kbld, waits=wts,
                incs=[(skb, 32), (svb, 16)], attach=False)
        kb_ready_h[h] = V(skb)
        vb_ready_h[h] = V(svb)

    def f_okones(g):
        g.memset(oneslo[:, :], 0.0)
        g.memset(oneslo[0:1, 0:64], 1.0)
        g.memset(oneshi[:, :], 0.0)
        return g.memset(oneshi[0:1, 64:128], 1.0)
    pl.step("gpsimd", f_okones, waits=[("s_kvst", MS["p1_kvst"])],
            incs=[("s_po", 1)])
    MS["okones_po"] = V("s_po")

    emit_kbld(0)
    emit_kbld(1)

    def emit_scores(gi):
        h, qt, pr, kc, kcmax = sev[gi]
        sl = h % 2
        cs = max(0, 16 * kc - 512 * qt)
        nq = 512 - cs
        qc0 = qt * 512 + cs
        pslot = gi % 4
        sb_ = psA if gi % 2 == 0 else psB

        def f_scores(t, h=h, pr=pr, sl=sl, kc=kc, cs=cs, nq=nq, qc0=qc0, sb_=sb_):
            ch = (2 * h + pr) * 1024
            t.matmul(sb_[:, 0, cs:512],
                     kb[sl][0:64, kc * 128:(kc + 1) * 128],
                     qsb[0:64, ch + qc0: ch + qc0 + nq],
                     start=True, stop=True, tile_position=(0, 0))
            return t.matmul(sb_[:, 1, cs:512],
                            kb[sl][64:128, kc * 128:(kc + 1) * 128],
                            qsb[64:128, ch + qc0: ch + qc0 + nq],
                            start=True, stop=True, tile_position=(64, 0))
        pl.step("tensor", f_scores,
                waits=[("s_kb%d" % sl, kb_ready_h[h]),
                       ("s_ac", MS["exps_prev"][gi % 2])],
                incs=[("s_pe", 1)])
        pe_s = V("s_pe")

        def f_exp(s, pslot=pslot, cs=cs, sb_=sb_):
            return s.activation(
                out=pb[pslot][:, :].rearrange("p (j n) -> p j n", j=2)[:, :, cs:512],
                in_=sb_[:, :, cs:512], func=AF.Exp, scale=0.125)
        pl.step("scalar", f_exp, waits=[("s_pe", pe_s)], incs=[("s_ac", 1)])
        MS["exps_prev"][gi % 2] = V("s_ac")
        exp_hist[gi] = V("s_ac")

        if 16 * kc >= 512 * qt:
            def f_mask(g, pslot=pslot, cs=cs, pr=pr):
                pv = pb[pslot][:, :].rearrange("p (j n) -> p j n", j=2)
                return g.tensor_tensor(
                    pv[:, :, cs:cs + 16], pv[:, :, cs:cs + 16],
                    maskq4[:, :].rearrange("p (j n) -> p j n", j=4)[:, 2 * pr:2 * pr + 2, :],
                    ALU.mult)
            pl.step("gpsimd", f_mask, waits=[("s_ac", V("s_ac"))],
                    incs=[("s_po", 1)])
        mask_hist[gi] = V("s_po")

    def emit_pv(gi):
        h, qt, pr, kc, kcmax = sev[gi]
        sl = h % 2
        pcs = max(0, 16 * kc - 512 * qt)
        ppslot = gi % 4
        first = (kc == 0)
        last = (kc == kcmax - 1)
        obank = psC if pr == 0 else psD

        ebank_pv = psE if pr == 0 else psF

        def f_pv(t, sl=sl, pcs=pcs, ppslot=ppslot, first=first,
                 last=last, kcm=kc, obank=obank, ebank_pv=ebank_pv):
            pv = pb[ppslot][:, :].rearrange("p (j n) -> p j n", j=2)
            vw = vb[sl][:, kcm * 65:(kcm + 1) * 65]
            t.matmul(obank[0:65, pcs:512], vw, pv[:, 0, pcs:512],
                     start=first, stop=last, tile_position=(0, 0),
                     skip_group_check=True)
            return t.matmul(ebank_pv[0:65, pcs:512], vw, pv[:, 1, pcs:512],
                            start=first, stop=last, tile_position=(0, 0),
                            skip_group_check=True)
        wts = [("s_ac", exp_hist[gi]), ("s_vb%d" % sl, vb_ready_h[h]),
               ("s_po", mask_hist[gi])]
        if first:
            wts.append(("s_dv", MS["oev_guard"][pr]))
        pl.step("tensor", f_pv, waits=wts, incs=[("s_pe", 1)])

        if last:
            ebank = psE if pr == 0 else psF

            def f_den(v, h=h, qt=qt, pr=pr, obank=obank, ebank=ebank):
                c0 = (2 * h + pr) * 1024 + qt * 512
                v.tensor_copy(out=osb[0:64, c0:c0 + 512], in_=obank[0:64, :])
                v.tensor_copy(out=oshift[:, :], in_=ebank[0:64, :])
                v.tensor_copy(out=drow[0:1, 0:512], in_=obank[64:65, :])
                return v.tensor_copy(out=drow[0:1, 512:1024], in_=ebank[64:65, :])
            pl.step("vector", f_den, waits=[("s_pe", V("s_pe"))], incs=[("s_dv", 1)])
            den_dv = V("s_dv")

            def f_bcsh(t, obank=obank, ebank=ebank):
                t.matmul(obank[:, :], oneslo[:, :], drow[0:1, 0:512],
                         start=True, stop=False, skip_group_check=True)
                t.matmul(obank[:, :], oneshi[:, :], drow[0:1, 512:1024],
                         start=False, stop=True, skip_group_check=True)
                return t.matmul(ebank[:, :], shfh[:, :], oshift[:, :],
                                start=True, stop=True, skip_group_check=True)
            pl.step("tensor", f_bcsh,
                    waits=[("s_dv", den_dv), ("s_po", MS["okones_po"])],
                    incs=[("s_pe", 1)])

            def f_n1(v, h=h, qt=qt, pr=pr, obank=obank, ebank=ebank):
                c0 = (2 * h + pr) * 1024 + qt * 512
                v.reciprocal(rdb[:, :], obank[:, :])
                v.tensor_tensor(osb[0:64, c0:c0 + 512], osb[0:64, c0:c0 + 512],
                                rdb[0:64, :], ALU.mult)
                return v.tensor_tensor(osb[64:128, c0:c0 + 512], ebank[64:128, :],
                                       rdb[64:128, :], ALU.mult)
            pl.step("vector", f_n1, waits=[("s_pe", V("s_pe"))], incs=[("s_dv", 1)])
            MS["bcsh_guard"] = V("s_dv")
            MS["oev_guard"][pr] = V("s_dv")

            if qt == 1 and pr == 1:
                MS["attn_pe_h%d" % h] = V("s_pe")
                if h + 2 < HKV:
                    emit_kbld(h + 2)

    for gi in range(NEV + 3):
        if gi < NEV:
            emit_scores(gi)
        if gi >= 3:
            emit_pv(gi - 3)
    MS["attn_pe"] = V("s_pe")
    MS["attn_dv"] = V("s_dv")
    dbg_dump("o_dbg", osb[:, :], [("s_dv", MS["attn_dv"])])

    def f_xbld(sy):
        return sy.dma_start(out=xb[:, :].rearrange("p (g t) -> p g t", g=16),
                            in_=xq.rearrange("(g p) t -> p g t", p=128))
    pl.step("sync", f_xbld, waits=[("s_pe", MS["attn_pe"]), ("s_dv", MS["attn_dv"])],
            incs=[("s_xb", 16)])

    def emit_wo(g16):
        sl = g16 % 2
        sw = "s_wo%d" % sl

        def f_wold(g, g16=g16, sl=sl, sw=sw):
            return g.dma_start(
                out=woh[sl][:, :].rearrange("p (g n) -> p g n", g=16),
                in_=wo[:, g16 * 128:(g16 + 1) * 128].rearrange("(g p) n -> p g n", p=128))
        wts = [(sw, V(sw)), ("s_dv", MS["ln1q_dv"])]
        if g16 >= 2:
            wts.append(("s_pe", MS["wo_pe_%d" % (g16 - 2)]))
        pl.step("gpsimd", f_wold, waits=wts, incs=[(sw, 16)])

        def f_womm(t, g16=g16, sl=sl):
            pm = psA if g16 % 2 == 0 else psB
            for t2 in range(2):
                for dc in range(G16):
                    i = t.matmul(pm[:, t2, :], woh[sl][:, dc * 128:(dc + 1) * 128],
                                 osb[:, dc * 1024 + t2 * 512: dc * 1024 + (t2 + 1) * 512],
                                 start=(dc == 0), stop=(dc == 15))
            return i
        wts = [(sw, V(sw)), ("s_dv", MS["attn_dv"])]
        if g16 >= 2:
            wts.append(("s_dv", MS["wo_dv_%d" % (g16 - 2)]))
        pl.step("tensor", f_womm, waits=wts, incs=[("s_pe", 1)])
        MS["wo_pe_%d" % g16] = V("s_pe")

        def f_woev(v, g16=g16):
            pm = psA if g16 % 2 == 0 else psB
            return v.tensor_tensor(xb[:, g16 * 1024:(g16 + 1) * 1024],
                                   pm[:, :, :].rearrange("p a b -> p (a b)"),
                                   xb[:, g16 * 1024:(g16 + 1) * 1024], ALU.add)
        pl.step("vector", f_woev, waits=[("s_pe", V("s_pe")), ("s_xb", 16)],
                incs=[("s_dv", 1)])
        MS["wo_dv_%d" % g16] = V("s_dv")

    def emit_rstat(g16):
        def f_rh(g, g16=g16):
            g.tensor_copy(out=rh3[:, :], in_=xb[:, g16 * 1024:(g16 + 1) * 1024])
            return g.tensor_mul(rsq3[:, :], rh3[:, :], rh3[:, :])
        wts = [("s_dv", MS["wo_dv_%d" % g16])]
        if g16 >= 1:
            wts.append(("s_pe", MS["rstat_pe_%d" % (g16 - 1)]))
        pl.step("gpsimd", f_rh, waits=wts, incs=[("s_po", 1)])

        wts = [("s_po", V("s_po"))]
        if g16 == 0:
            wts.append(("s_dv", MS["attn_dv"]))

        def f_rstat(t, g16=g16):
            for t2 in range(2):
                pm = psE if t2 == 0 else psF
                t.matmul(pm[0:64, :], ones64[:, :], rh3[:, t2 * 512:(t2 + 1) * 512],
                         start=(g16 == 0), stop=(g16 == 15))
            for t2 in range(2):
                pm = psC if t2 == 0 else psD
                i = t.matmul(pm[0:64, :], ones64[:, :], rsq3[:, t2 * 512:(t2 + 1) * 512],
                             start=(g16 == 0), stop=(g16 == 15))
            return i
        pl.step("tensor", f_rstat, waits=wts, incs=[("s_pe", 1)])
        MS["rstat_pe_%d" % g16] = V("s_pe")

    for g16 in range(G16 + 1):
        if g16 < G16:
            emit_wo(g16)
        if g16 >= 1:
            emit_rstat(g16 - 1)

    def f_r3rows(v):
        for t2 in range(2):
            pm = psE if t2 == 0 else psF
            pv2 = psC if t2 == 0 else psD
            v.tensor_scalar_mul(m3[0:1, t2 * 512:(t2 + 1) * 512], pm[0:1, :], 1.0 / D)
            v.tensor_scalar_mul(v3[0:1, t2 * 512:(t2 + 1) * 512], pv2[0:1, :], 1.0 / D)
            v.tensor_mul(pm[0:1, :], m3[0:1, t2 * 512:(t2 + 1) * 512],
                         m3[0:1, t2 * 512:(t2 + 1) * 512])
            i = v.tensor_tensor(v3[0:1, t2 * 512:(t2 + 1) * 512],
                                v3[0:1, t2 * 512:(t2 + 1) * 512], pm[0:1, :],
                                ALU.subtract)
        return v.tensor_scalar_add(v3[0:1, :], v3[0:1, :], EPS)
    pl.step("vector", f_r3rows, waits=[("s_pe", V("s_pe"))], incs=[("s_dv", 1)])

    def f_r3std(s):
        s.activation(out=sd3[0:1, :], in_=v3[0:1, :], func=AF.Ln)
        return s.activation(out=v3[0:1, :], in_=sd3[0:1, :], func=AF.Exp, scale=0.5)
    pl.step("scalar", f_r3std, waits=[("s_dv", V("s_dv"))], incs=[("s_ac", 1)])

    def f_hilo(v):
        v.scalar_tensor_tensor(hirow[0:1, :], v3[0:1, :], 0.33, m3[0:1, :],
                               ALU.mult, ALU.add)
        return v.scalar_tensor_tensor(lorow[0:1, :], v3[0:1, :], -0.33, m3[0:1, :],
                                      ALU.mult, ALU.add)
    pl.step("vector", f_hilo, waits=[("s_ac", V("s_ac"))], incs=[("s_dv", 1)])

    def f_hibc_prep(v):
        v.tensor_copy(out=bch[:, :], in_=hirow[0:1, :])
        return v.tensor_tensor(bcl[:, :], hirow[0:1, :], bch[:, :], ALU.subtract)
    pl.step("vector", f_hibc_prep, waits=[("s_dv", V("s_dv"))], incs=[("s_dv", 1)])

    def f_hibc(t):
        for t2 in range(2):
            pm = psE if t2 == 0 else psF
            t.matmul(pm[:, :], onesrh[:, :], bch[0:1, t2 * 512:(t2 + 1) * 512],
                     start=True, stop=False)
            i = t.matmul(pm[:, :], onesrh[:, :], bcl[0:1, t2 * 512:(t2 + 1) * 512],
                         start=False, stop=True)
        return i
    pl.step("tensor", f_hibc, waits=[("s_dv", V("s_dv"))], incs=[("s_pe", 1)])

    def f_lobc_prep(v):
        v.tensor_copy(out=bch[:, :], in_=lorow[0:1, :])
        return v.tensor_tensor(bcl[:, :], lorow[0:1, :], bch[:, :], ALU.subtract)
    pl.step("vector", f_lobc_prep, waits=[("s_pe", V("s_pe"))], incs=[("s_dv", 1)])

    def f_lobc(t):
        for t2 in range(2):
            pm = psC if t2 == 0 else psD
            t.matmul(pm[:, :], onesrh[:, :], bch[0:1, t2 * 512:(t2 + 1) * 512],
                     start=True, stop=False)
            i = t.matmul(pm[:, :], onesrh[:, :], bcl[0:1, t2 * 512:(t2 + 1) * 512],
                         start=False, stop=True)
        return i
    pl.step("tensor", f_lobc, waits=[("s_dv", V("s_dv"))], incs=[("s_pe", 1)])

    def f_hilobc_ev(v):
        v.tensor_copy(out=hibc[:, 0:512], in_=psE[:, :])
        v.tensor_copy(out=hibc[:, 512:1024], in_=psF[:, :])
        v.tensor_copy(out=lobc[:, 0:512], in_=psC[:, :])
        return v.tensor_copy(out=lobc[:, 512:1024], in_=psD[:, :])
    pl.step("vector", f_hilobc_ev, waits=[("s_pe", V("s_pe"))], incs=[("s_dv", 1)])
    MS["hilo_dv"] = V("s_dv")

    for g16 in range(G16):
        def f_g1tq(v, g16=g16):
            v.tensor_tensor(g2b[:, :], xb[:, g16 * 1024:(g16 + 1) * 1024],
                            lobc[:, :], ALU.is_lt)
            v.tensor_tensor(g1b[:, :], xb[:, g16 * 1024:(g16 + 1) * 1024],
                            hibc[:, :], ALU.is_gt)
            return v.tensor_tensor(tqsb[:, g16 * 1024:(g16 + 1) * 1024],
                                   g1b[:, :], g2b[:, :], ALU.subtract)
        pl.step("vector", f_g1tq, waits=[("s_dv", MS["hilo_dv"])],
                incs=[("s_dv", 1)])
        MS["tq_dv_%d" % g16] = V("s_dv")
    MS["tq_dv"] = V("s_dv")

    def emit_ffn_front(g16):
        sl = g16 % 2
        sw = "s_wf%d" % sl

        def f_wfld(g, g16=g16, sl=sl, sw=sw):
            g.dma_start(
                out=wfh[sl][:, :].rearrange("p (g n) -> p g n", g=16),
                in_=wft[:, g16 * 128:(g16 + 1) * 128].rearrange("(g p) n -> p g n", p=128)
            ).then_inc(sems[sw], 16)
            return g.dma_start(
                out=rrh[sl][:, :].rearrange("p (g n) -> p g n", g=16),
                in_=rrt[:, g16 * 128:(g16 + 1) * 128].rearrange("(g p) n -> p g n", p=128)
            ).then_inc(sems[sw], 16)
        wts = [(sw, V(sw))]
        if g16 >= 2:
            wts.append(("s_pe", MS["ffn_pe_%d" % (g16 - 2)]))
        pl.step("gpsimd", f_wfld, waits=wts, incs=[(sw, 32)], attach=False)

        def f_tern(v, sl=sl):
            v.tensor_single_scalar(tt1[:, :], wfh[sl][:, :], 0.2, ALU.is_gt)
            v.tensor_single_scalar(tt2[:, :], wfh[sl][:, :], -0.2, ALU.is_lt)
            v.tensor_tensor(tt1[:, :], tt1[:, :], tt2[:, :], ALU.subtract)
            return v.scalar_tensor_tensor(wfh[sl][:, :], tt1[:, :], 0.6,
                                          rrh[sl][:, :], ALU.mult, ALU.add)
        pl.step("vector", f_tern, waits=[(sw, V(sw))], incs=[("s_dv", 1)])
        MS["tern_dv_%d" % g16] = V("s_dv")

        def f_ffnmm(t, g16=g16, sl=sl):
            pm = psA if g16 % 2 == 0 else psB
            for t2 in range(2):
                for dc in range(G16):
                    i = t.matmul(pm[:, t2, :], wfh[sl][:, dc * 128:(dc + 1) * 128],
                                 tqsb[:, dc * 1024 + t2 * 512: dc * 1024 + (t2 + 1) * 512],
                                 start=(dc == 0), stop=(dc == 15))
            return i
        wts = [("s_dv", MS["tern_dv_%d" % g16]), ("s_dv", MS["tq_dv"])]
        if g16 >= 2:
            wts.append(("s_dv", MS["ffnev_dv_%d" % (g16 - 2)]))
        pl.step("tensor", f_ffnmm, waits=wts, incs=[("s_pe", 1)])
        MS["ffn_pe_%d" % g16] = V("s_pe")

    def emit_ffn_back(g16):
        sl = g16 % 2

        def f_ffnev(v, g16=g16, sl=sl):
            pm = psA if g16 % 2 == 0 else psB
            return v.scalar_tensor_tensor(outs[sl][:, :],
                                          pm[:, :, :].rearrange("p a b -> p (a b)"),
                                          0.5, xb[:, g16 * 1024:(g16 + 1) * 1024],
                                          ALU.mult, ALU.add)
        wts = [("s_pe", MS["ffn_pe_%d" % g16])]
        if g16 >= 2:
            wts.append(("s_st", MS["ost_%d" % (g16 - 2)]))
        pl.step("vector", f_ffnev, waits=wts, incs=[("s_dv", 1)])
        MS["ffnev_dv_%d" % g16] = V("s_dv")

        def f_ost(sy, g16=g16, sl=sl):
            return sy.dma_start(out=out[g16 * 128:(g16 + 1) * 128, :],
                                in_=outs[sl][:, :])
        pl.step("sync", f_ost, waits=[("s_dv", V("s_dv"))], incs=[("s_st", 16)])
        MS["ost_%d" % g16] = V("s_st")

    for g16 in range(G16 + 1):
        if g16 < G16:
            emit_ffn_front(g16)
        if g16 >= 1:
            emit_ffn_back(g16 - 1)

    if "k_dbg" in dbg_out:
        pl.step("sync", lambda sy: sy.dma_start(
            out=dbg_out["k_dbg"][:, :],
            in_=kfull.rearrange("h d c k -> (h d) (c k)")),
            waits=[("s_kvst", MS["p1_kvst"])], incs=[("s_st", 16)])
    if "v_dbg" in dbg_out:
        pl.step("sync", lambda sy: sy.dma_start(
            out=dbg_out["v_dbg"][:, :],
            in_=vfull.rearrange("h p c d -> (h p) (c d)")),
            waits=[("s_kvst", MS["p1_kvst"])], incs=[("s_st", 16)])
    if "tq_dbg" in dbg_out:
        pl.step("sync", lambda sy: sy.dma_start(
            out=dbg_out["tq_dbg"][:, :], in_=tqsb[:, :]),
            waits=[("s_dv", MS["tq_dv"])], incs=[("s_st", 16)])
    st_total = V("s_st")
    pl.step("sync", lambda sy: sy.nop(), waits=[("s_st", st_total)])

    for (_e, _f, _w, _i, _a) in pl.steps:
        for (sname, val) in _w:
            assert val <= pl.cnt.get(sname, 0), (
                "unsatisfiable wait %s >= %d (max %d)" % (sname, val, pl.cnt.get(sname, 0)))

    with nc.Block() as block:
        def make(engname):
            def body(eng):
                last_wait = {}
                for (e, fn, waits, incs, attach) in pl.steps:
                    if e != engname:
                        continue
                    for (sname, val) in waits:
                        if last_wait.get(sname, -1) < val:
                            eng.wait_ge(sems[sname], val)
                            last_wait[sname] = val
                    inst = fn(eng)
                    if attach:
                        for (sname, amt) in incs:
                            inst = inst.then_inc(sems[sname], amt)
            return body
        block.sync(make("sync"))
        block.tensor(make("tensor"))
        block.vector(make("vector"))
        block.scalar(make("scalar"))
        block.gpsimd(make("gpsimd"))

    for ctx in reversed(sem_ctxs):
        ctx.__exit__(None, None, None)

    return nc


def make_inputs(L0_attn_x, L0_attn_Wq, L0_attn_Wk, L0_attn_Wv, L0_attn_Wo,
                L0_ffn_weight, L0_ffn_rram_variation):
    xT = np.ascontiguousarray(np.asarray(L0_attn_x, np.float32)[0].T)
    wftT = np.ascontiguousarray(np.asarray(L0_ffn_weight, np.float32).T)
    rrtT = np.ascontiguousarray(np.asarray(L0_ffn_rram_variation, np.float32).T)
    in_maps = []
    for c in range(NCORE):
        kk = np.arange(128)[:, None]
        jj = np.arange(16)[None, :]
        mqc = (kk <= 8 * jj + c).astype(np.float32)
        in_maps.append({
            "xq": np.ascontiguousarray(xT[:, c::8]),
            "xf": xT,
            "wq": np.asarray(L0_attn_Wq, np.float32),
            "wk": np.asarray(L0_attn_Wk, np.float32),
            "wv": np.asarray(L0_attn_Wv, np.float32),
            "wo": np.asarray(L0_attn_Wo, np.float32),
            "wft": wftT,
            "rrt": rrtT,
            "mq": mqc,
            "shf": np.concatenate([np.zeros((64, 64), np.float32),
                                   np.eye(64, dtype=np.float32)], axis=1),
        })
    return in_maps


def kernel(L0_attn_x, L0_attn_Wq, L0_attn_Wk, L0_attn_Wv, L0_attn_Wo,
           L0_ffn_weight, L0_ffn_rram_variation):
    in_maps = make_inputs(L0_attn_x, L0_attn_Wq, L0_attn_Wk, L0_attn_Wv,
                          L0_attn_Wo, L0_ffn_weight, L0_ffn_rram_variation)
    nc = build()
    res = run_bass_kernel_spmd(nc, in_maps, core_ids=list(range(NCORE)))
    y = np.empty((1, S, D), np.float32)
    for c in range(NCORE):
        y[0, c::8, :] = np.asarray(res.results[c]["out"], np.float32).T
    return y


# version 17
# speedup vs baseline: 1.2507x; 1.2507x over previous
import numpy as np
import concourse.bass as bass
import concourse.mybir as mybir
from concourse.bass_utils import run_bass_kernel_spmd

F32 = mybir.dt.float32
F16 = mybir.dt.float16
AF = mybir.ActivationFunctionType
ALU = mybir.AluOpType

NCORE = 8
S = 8192
D = 2048
TQ = 1024
DKV = 512
HKV = 8
HD = 64
G16 = 16
NTC = 16
TCW = 512
KCN = 64
EPS = 1e-5

QS = 0
OS = 32768
XH0, XH1, XH2 = 65536, 81920, 98304
XSQ = 114688
KSB0, KSB1 = 163840, 164864
VSTO = 8192
RBCK = 169984
VR0, VR1 = 0, 2048
MR0, MR1 = 4096, 6144
MQ32O, VQ32O = 12480, 16576
QSQ2O = 20672
QLNO = 24768
KB0, KB1 = 65536, 81920
VB0, VB1 = 98304, 106624
PB0, PB1, PB2, PB3 = 114944, 116992, 119040, 121088
XG0 = 123136
RDO = 163840
OSHIFT = 165888
DROW = 166912
SHFO = 212416
ONESLO = 123136
XBO = 65536
WKHO, WVHO = 131072, 147456
MBCQO, RBCQO = 131072, 135168
WOH0, WOH1 = 131072, 135168
WFH0, WFH1 = 139264, 143360
RRH0, RRH1 = 147456, 151552
TT1O, TT2O = 155648, 159744
WQH0, WQH1 = 163840, 167936
HIROWO, LOROWO = 163840, 167936
OUTS0, OUTS1 = 163840, 167936
MROWO = 172032
HIBCO, LOBCO = 172032, 176128
G1BO, G2BO = 180224, 182272
RH3O, RSQ3O = 184320, 186368
RSTDROWO = 188416
M3O, V3O, SD3O = 188416, 192512, 196608
ONES64O = 204800
ONESCO = 204928
ONESRHO = 204992
ONESRFO = 205248
MASKQO = 205760
MASKQ4O = 205888
RSTDCOLO = 206016
WSKNO = 206272
WSVNO = 207296
RS320, RS321 = 208320, 210368
BCH1O, BCH2O = 184320, 186368


class Plan:
    def __init__(self):
        self.steps = []
        self.cnt = {}

    def v(self, sem):
        return self.cnt.get(sem, 0)

    def step(self, eng, fn, waits=(), incs=(), attach=True):
        self.steps.append(
            (eng, fn, [(s, x) for (s, x) in waits if x > 0], list(incs), attach))
        for s, amt in incs:
            self.cnt[s] = self.cnt.get(s, 0) + amt


def build(dbg=()):
    nc = bass.Bass("TRN2", target_bir_lowering=False, debug=False)

    xq = nc.dram_tensor("xq", [D, TQ], F32, kind="ExternalInput").ap()
    xf = nc.dram_tensor("xf", [D, S], F32, kind="ExternalInput").ap()
    wq = nc.dram_tensor("wq", [D, D], F32, kind="ExternalInput").ap()
    wk = nc.dram_tensor("wk", [D, DKV], F32, kind="ExternalInput").ap()
    wv = nc.dram_tensor("wv", [D, DKV], F32, kind="ExternalInput").ap()
    wo = nc.dram_tensor("wo", [D, D], F32, kind="ExternalInput").ap()
    wft = nc.dram_tensor("wft", [D, D], F32, kind="ExternalInput").ap()
    rrt = nc.dram_tensor("rrt", [D, D], F32, kind="ExternalInput").ap()
    mq = nc.dram_tensor("mq", [128, 16], F32, kind="ExternalInput").ap()
    shf = nc.dram_tensor("shf", [64, 128], F32, kind="ExternalInput").ap()
    out = nc.dram_tensor("out", [D, TQ], F32, kind="ExternalOutput").ap()
    kfull = nc.dram_tensor("kfull", [HKV, HD, KCN, 128], F16, kind="Internal").ap()
    vfull = nc.dram_tensor("vfull", [HKV, 128, KCN, HD + 1], F16, kind="Internal").ap()

    dbg_out = {}
    for name, shape, dt in dbg:
        dbg_out[name] = nc.dram_tensor(name, shape, dt, kind="ExternalOutput").ap()

    def dbg_dump(name, src_ap, waits):
        if name in dbg_out:
            def f_dump(sy, name=name, src_ap=src_ap):
                return sy.dma_start(out=dbg_out[name][:, :], in_=src_ap)
            pl.step("sync", f_dump, waits=waits, incs=[("s_st", 16)])

    SBASE = 16640

    def sb(name, off, shape, dt):
        return nc.alloc_sbuf_tensor_at(name, shape, dt, offset=off + SBASE).ap()

    qsb = sb("qsb", QS, [128, 16384], F16)
    tqsb = sb("tqsb", QS, [128, 16384], F16)
    xqh = sb("xqh", OS, [128, 16384], F16)
    osb = sb("osb", OS, [128, 16384], F16)
    xh = [sb("xh0", XH0, [128, 8192], F16), sb("xh1", XH1, [128, 8192], F16),
          sb("xh2", XH2, [128, 8192], F16)]
    xsq = sb("xsq", XSQ, [128, 8192], F16)
    ksb = [sb("ksb0", KSB0, [128, 512], F16), sb("ksb1", KSB1, [128, 512], F16)]
    vst_full = sb("vst", VSTO, [128, 2144], F16)
    vst = vst_full[:, 0:2080]
    vrow = [sb("vr0", VR0, [1, 512], F32), sb("vr1", VR1, [1, 512], F32)]
    mrow32 = [sb("mr0", MR0, [1, 512], F32), sb("mr1", MR1, [1, 512], F32)]
    rbck = sb("rbck", RBCK, [128, 512], F32)
    mq32 = sb("mq32", MQ32O, [1, 1024], F32)
    vq32 = sb("vq32", VQ32O, [1, 1024], F32)
    qsq2 = sb("qsq2", QSQ2O, [128, 2048], F16)
    qln = sb("qln", QLNO, [1, 1024], F32)
    kb = [sb("kb0", KB0, [128, 8192], F16), sb("kb1", KB1, [128, 8192], F16)]
    vb = [sb("vb0", VB0, [128, 4160], F16), sb("vb1", VB1, [128, 4160], F16)]
    pb = [sb("pb0", PB0, [128, 1024], F16), sb("pb1", PB1, [128, 1024], F16),
          sb("pb2", PB2, [128, 1024], F16), sb("pb3", PB3, [128, 1024], F16)]
    rdb = sb("rdb", RDO, [128, 512], F32)
    oshift = sb("oshift", OSHIFT, [64, 512], F16)
    drow = sb("drow", DROW, [1, 1024], F16)
    shfh = sb("shfh", SHFO, [64, 128], F16)
    oneslo = sb("oneslo", ONESLO, [1, 128], F16)
    oneshi = sb("oneshi", ONESLO + 256, [1, 128], F16)
    xb = sb("xb", XBO, [128, 16384], F32)
    wkh = sb("wkh", WKHO, [128, 8192], F16)
    wvh = sb("wvh", WVHO, [128, 8192], F16)
    mbcq = sb("mbcq", MBCQO, [128, 1024], F32)
    rbcq = sb("rbcq", RBCQO, [128, 1024], F32)
    woh = [sb("woh0", WOH0, [128, 2048], F16), sb("woh1", WOH1, [128, 2048], F16)]
    wfh = [sb("wfh0", WFH0, [128, 2048], F16), sb("wfh1", WFH1, [128, 2048], F16)]
    rrh = [sb("rrh0", RRH0, [128, 2048], F16), sb("rrh1", RRH1, [128, 2048], F16)]
    tt1 = sb("tt1", TT1O, [128, 2048], F16)
    tt2 = sb("tt2", TT2O, [128, 2048], F16)
    wqh = [sb("wqh0", WQH0, [128, 2048], F16), sb("wqh1", WQH1, [128, 2048], F16)]
    hirow = sb("hirow", HIROWO, [1, 1024], F32)
    lorow = sb("lorow", LOROWO, [1, 1024], F32)
    outs = [sb("outs0", OUTS0, [128, 1024], F32), sb("outs1", OUTS1, [128, 1024], F32)]
    mrowh = sb("mrowh", MROWO, [1, 8192], F16)
    hibc = sb("hibc", HIBCO, [128, 1024], F32)
    lobc = sb("lobc", LOBCO, [128, 1024], F32)
    g1b = sb("g1b", G1BO, [128, 1024], F16)
    g2b = sb("g2b", G2BO, [128, 1024], F16)
    rh3 = sb("rh3", RH3O, [128, 1024], F16)
    rsq3 = sb("rsq3", RSQ3O, [128, 1024], F16)
    rstdrowh = sb("rstdrowh", RSTDROWO, [1, 8192], F16)
    m3 = sb("m3", M3O, [1, 1024], F32)
    v3 = sb("v3", V3O, [1, 1024], F32)
    sd3 = sb("sd3", SD3O, [1, 1024], F32)
    ones64 = sb("ones64", ONES64O, [128, 64], F16)
    onesc = sb("onesc", ONESCO, [128, 1], F16)
    onesrh = sb("onesrh", ONESRHO, [1, 128], F16)
    onesrf = sb("onesrf", ONESRFO, [1, 128], F32)
    maskq = sb("maskq_sb", MASKQO, [128, 16], F32)
    maskq4 = sb("maskq4", MASKQ4O, [128, 64], F16)
    rstdcol = sb("rstdcol", RSTDCOLO, [128, 64], F32)
    wskn = sb("wskn", WSKNO, [1, 512], F16)
    wsvn = sb("wsvn", WSVNO, [1, 512], F16)
    rs32 = [sb("rs320", RS320, [1, 512], F32), sb("rs321", RS321, [1, 512], F32)]
    bch = sb("bch", BCH1O, [1, 1024], F16)
    bcl = sb("bcl", BCH2O, [1, 1024], F16)

    psA = nc.alloc_psum_tensor("psA", [128, 2, 512], F32).ap()
    psB = nc.alloc_psum_tensor("psB", [128, 2, 512], F32).ap()
    psC = nc.alloc_psum_tensor("psC", [128, 512], F32).ap()
    psD = nc.alloc_psum_tensor("psD", [128, 512], F32).ap()
    psE = nc.alloc_psum_tensor("psE", [128, 512], F32).ap()
    psF = nc.alloc_psum_tensor("psF", [128, 512], F32).ap()

    semnames = ["s_x0", "s_x1", "s_x2", "s_xq", "s_ms", "s_wk", "s_q0", "s_q1",
                "s_kb0", "s_kb1", "s_vb0", "s_vb1", "s_xb",
                "s_wo0", "s_wo1", "s_wf0", "s_wf1", "s_kvst", "s_st",
                "s_pe", "s_ac", "s_dv", "s_po"]
    sems = {}
    sem_ctxs = []
    for n in semnames:
        ctx = nc.semaphore(n)
        sems[n] = ctx.__enter__()
        sem_ctxs.append(ctx)

    pl = Plan()
    V = pl.v
    MS = {}

    def f_const(g):
        g.memset(ones64[:, :], 1.0)
        g.memset(onesc[:, :], 1.0)
        g.memset(onesrh[:, :], 1.0)
        return g.memset(onesrf[:, :], 1.0)
    pl.step("gpsimd", f_const, incs=[("s_po", 1)])

    pl.step("gpsimd", lambda g: g.dma_start(out=maskq[:, :], in_=mq[:, :]),
            incs=[("s_ms", 16)])

    pl.step("gpsimd", lambda g: g.dma_start(out=shfh[:, :], in_=shf[:, :]),
            incs=[("s_ms", 16)])

    def f_vones(g):
        return g.memset(vst_full[:, :], 1.0)
    pl.step("gpsimd", f_vones, incs=[("s_po", 1)])

    def f_mask4(g):
        for j in range(4):
            i = g.tensor_copy(out=maskq4[:, j * 16:(j + 1) * 16], in_=maskq[:, :])
        return i
    pl.step("gpsimd", f_mask4, waits=[("s_ms", 16)], incs=[("s_po", 1)])

    def f_xqload(g):
        return g.dma_start(out=xqh[:, :].rearrange("p (g t) -> p g t", g=16),
                           in_=xq.rearrange("(g p) t -> p g t", p=128))
    pl.step("gpsimd", f_xqload, incs=[("s_xq", 16)])

    def f_wkload(g):
        g.dma_start(out=wkh[:, :].rearrange("p (g n) -> p g n", g=16),
                    in_=wk.rearrange("(g p) n -> p g n", p=128)
                    ).then_inc(sems["s_wk"], 16)
        return g.dma_start(out=wvh[:, :].rearrange("p (g n) -> p g n", g=16),
                           in_=wv.rearrange("(g p) n -> p g n", p=128)
                           ).then_inc(sems["s_wk"], 16)
    pl.step("gpsimd", f_wkload, incs=[("s_wk", 32)], attach=False)

    def f_wsum_mm(t):
        for dc in range(G16):
            t.matmul(psC[0:64, :], ones64[:, :], wkh[:, dc * 512:(dc + 1) * 512],
                     start=(dc == 0), stop=(dc == 15))
        for dc in range(G16):
            i = t.matmul(psD[0:64, :], ones64[:, :], wvh[:, dc * 512:(dc + 1) * 512],
                         start=(dc == 0), stop=(dc == 15))
        return i
    pl.step("tensor", f_wsum_mm, waits=[("s_wk", 32), ("s_po", 1)],
            incs=[("s_pe", 1)])

    def f_wsum_ev(v):
        v.tensor_scalar_mul(wskn[:, :], psC[0:1, :], -1.0)
        return v.tensor_scalar_mul(wsvn[:, :], psD[0:1, :], -1.0)
    pl.step("vector", f_wsum_ev, waits=[("s_pe", V("s_pe"))], incs=[("s_dv", 1)])
    MS["wsum_dv"] = V("s_dv")

    for g16 in range(G16):
        def f_qsq(v, g16=g16):
            sl2 = (g16 % 2) * 1024
            return v.tensor_mul(qsq2[:, sl2:sl2 + 1024],
                                xqh[:, g16 * 1024:(g16 + 1) * 1024],
                                xqh[:, g16 * 1024:(g16 + 1) * 1024])
        wts = [("s_xq", 16)]
        if g16 >= 2:
            wts.append(("s_pe", MS["qstat_pe_%d" % (g16 - 2)]))
        pl.step("vector", f_qsq, waits=wts, incs=[("s_dv", 1)])

        def f_qstat(t, g16=g16):
            sl2 = (g16 % 2) * 1024
            for t2 in range(2):
                t.matmul(psA[0:64, t2, :], ones64[:, :],
                         xqh[:, g16 * 1024 + t2 * 512: g16 * 1024 + (t2 + 1) * 512],
                         start=(g16 == 0), stop=(g16 == 15))
            for t2 in range(2):
                i = t.matmul(psB[0:64, t2, :], ones64[:, :],
                             qsq2[:, sl2 + t2 * 512: sl2 + (t2 + 1) * 512],
                             start=(g16 == 0), stop=(g16 == 15))
            return i
        pl.step("tensor", f_qstat, waits=[("s_dv", V("s_dv")), ("s_po", 1)],
                incs=[("s_pe", 1)])
        MS["qstat_pe_%d" % g16] = V("s_pe")

    def f_qrows(v):
        for t2 in range(2):
            v.tensor_scalar_mul(mq32[0:1, t2 * 512:(t2 + 1) * 512],
                                psA[0:1, t2, :], 1.0 / D)
            v.tensor_scalar_mul(vq32[0:1, t2 * 512:(t2 + 1) * 512],
                                psB[0:1, t2, :], 1.0 / D)
            v.tensor_mul(psA[0:1, t2, :], mq32[0:1, t2 * 512:(t2 + 1) * 512],
                         mq32[0:1, t2 * 512:(t2 + 1) * 512])
            i = v.tensor_tensor(vq32[0:1, t2 * 512:(t2 + 1) * 512],
                                vq32[0:1, t2 * 512:(t2 + 1) * 512],
                                psA[0:1, t2, :], ALU.subtract)
        return v.tensor_scalar_add(vq32[0:1, :], vq32[0:1, :], EPS)
    pl.step("vector", f_qrows, waits=[("s_pe", V("s_pe"))], incs=[("s_dv", 1)])
    MS["qrows_dv"] = V("s_dv")
    dbg_dump("mq_dbg", mq32[:, :], [("s_dv", MS["qrows_dv"])])

    def f_qrstd(s):
        s.activation(out=qln[0:1, :], in_=vq32[0:1, :], func=AF.Ln)
        return s.activation(out=vq32[0:1, :], in_=qln[0:1, :], func=AF.Exp,
                            scale=-0.5)
    pl.step("scalar", f_qrstd, waits=[("s_dv", V("s_dv"))], incs=[("s_ac", 1)])
    MS["qrstd_ac"] = V("s_ac")
    dbg_dump("rq_dbg", vq32[:, :], [("s_ac", MS["qrstd_ac"])])

    pe_proj_done = [0] * NTC

    def emit_stats(tc):
        sl = tc % 3
        sx = "s_x%d" % sl
        wts = [(sx, V(sx))]
        if tc >= 3:
            wts.append(("s_pe", pe_proj_done[tc - 3]))

        def f_xload(g, tc=tc, sl=sl):
            return g.dma_start(
                out=xh[sl][:, :].rearrange("p (g t) -> p g t", g=16),
                in_=xf[:, tc * TCW:(tc + 1) * TCW].rearrange("(g p) t -> p g t", p=128))
        pl.step("gpsimd", f_xload, waits=wts, incs=[(sx, 16)])

        wts = [(sx, V(sx))]
        if tc >= 1:
            wts.append(("s_pe", MS["stats_pe_%d" % (tc - 1)]))

        def f_sq(g, sl=sl):
            return g.tensor_mul(xsq[:, :], xh[sl][:, :], xh[sl][:, :])
        pl.step("gpsimd", f_sq, waits=wts, incs=[("s_po", 1)])

        wts = [("s_po", V("s_po"))]
        if tc >= 1:
            wts.append(("s_dv", MS["rows_dv_%d" % (tc - 1)]))

        def f_stats(t, sl=sl):
            for g in range(G16):
                t.matmul(psE[0:64, :], ones64[:, :], xh[sl][:, g * 512:(g + 1) * 512],
                         start=(g == 0), stop=(g == 15))
            for g in range(G16):
                i = t.matmul(psF[0:64, :], ones64[:, :], xsq[:, g * 512:(g + 1) * 512],
                             start=(g == 0), stop=(g == 15))
            return i
        pl.step("tensor", f_stats, waits=wts, incs=[("s_pe", 1)])
        MS["stats_pe_%d" % tc] = V("s_pe")

        rl = tc % 2

        def f_rows(v, tc=tc, rl=rl):
            v.tensor_scalar_mul(mrow32[rl][:, :], psE[0:1, :], 1.0 / D)
            v.tensor_scalar_mul(vrow[rl][:, :], psF[0:1, :], 1.0 / D)
            v.tensor_mul(psE[0:1, :], mrow32[rl][:, :], mrow32[rl][:, :])
            v.tensor_tensor(vrow[rl][:, :], vrow[rl][:, :], psE[0:1, :], ALU.subtract)
            v.tensor_scalar_add(vrow[rl][:, :], vrow[rl][:, :], EPS)
            return v.tensor_copy(out=mrowh[0:1, tc * TCW:(tc + 1) * TCW],
                                 in_=mrow32[rl][:, :])
        pl.step("vector", f_rows, waits=[("s_pe", V("s_pe"))], incs=[("s_dv", 1)])
        MS["rows_dv_%d" % tc] = V("s_dv")

        def f_rstd(s, tc=tc, rl=rl):
            s.activation(out=mrow32[rl][:, :], in_=vrow[rl][:, :], func=AF.Ln)
            s.activation(out=rs32[rl][:, :], in_=mrow32[rl][:, :], func=AF.Exp,
                         scale=-0.5)
            return s.activation(out=rstdrowh[0:1, tc * TCW:(tc + 1) * TCW],
                                in_=rs32[rl][:, :], func=AF.Copy)
        pl.step("scalar", f_rstd, waits=[("s_dv", V("s_dv"))], incs=[("s_ac", 1)])
        MS["rstd_ac_%d" % tc] = V("s_ac")

    def emit_proj(tc):
        sl = tc % 3
        rl = tc % 2
        wts = [("s_ac", MS["rstd_ac_%d" % tc])]
        if tc >= 1:
            wts.append(("s_dv", MS["tcol_dv_%d" % (tc - 1)]))

        def f_tcol(t, tc=tc, rl=rl):
            for ts in range(4):
                t.transpose(psD[:, ts:ts + 1],
                            rs32[rl][0:1, ts * 128:(ts + 1) * 128],
                            onesrf[0:1, 0:1])
            return t.matmul(psC[:, :], onesrh[:, :],
                            rstdrowh[0:1, tc * TCW:(tc + 1) * TCW],
                            start=True, stop=True)
        pl.step("tensor", f_tcol, waits=wts, incs=[("s_pe", 1)])

        def f_tcol_ev(v, tc=tc):
            v.tensor_copy(out=rstdcol[:, tc * 4:tc * 4 + 4], in_=psD[:, 0:4])
            return v.tensor_copy(out=rbck[:, :], in_=psC[:, :])
        pl.step("vector", f_tcol_ev,
                waits=[("s_pe", V("s_pe")), ("s_dv", MS["wsum_dv"])],
                incs=[("s_dv", 1)])
        MS["tcol_dv_%d" % tc] = V("s_dv")

        for b in range(4):
            pslot = psA[:, b % 2, :]

            def f_kmm(t, tc=tc, sl=sl, b=b, pslot=pslot):
                for dc in range(G16):
                    t.matmul(pslot,
                             wkh[:, dc * 512 + b * 128: dc * 512 + (b + 1) * 128],
                             xh[sl][:, dc * 512:(dc + 1) * 512],
                             start=(dc == 0), stop=False)
                return t.matmul(pslot, wskn[0:1, b * 128:(b + 1) * 128],
                                mrowh[0:1, tc * TCW:(tc + 1) * TCW],
                                start=False, stop=True)
            wts = []
            prev = ("kev_dv_%d_%d" % (tc, b - 2)) if b >= 2 else \
                   ("kev_dv_%d_%d" % (tc - 1, b + 2)) if tc >= 1 else None
            if prev:
                wts.append(("s_dv", MS[prev]))
            if tc == 0 and b == 0:
                wts.append(("s_dv", MS["qrows_dv"]))
            pl.step("tensor", f_kmm, waits=wts, incs=[("s_pe", 1)])
            pe_k = V("s_pe")

            def f_kev(v, b=b, pslot=pslot):
                return v.tensor_tensor(ksb[b % 2][:, :], pslot, rbck[:, :], ALU.mult)
            wts = [("s_pe", pe_k), ("s_dv", MS["tcol_dv_%d" % tc])]
            prevst = ("kst_%d_%d" % (tc, b - 2)) if b >= 2 else \
                     ("kst_%d_%d" % (tc - 1, b + 2)) if tc >= 1 else None
            if prevst:
                wts.append(("s_kvst", MS[prevst]))
            pl.step("vector", f_kev, waits=wts, incs=[("s_dv", 1)])
            MS["kev_dv_%d_%d" % (tc, b)] = V("s_dv")

            def f_kst(sy, tc=tc, b=b):
                for e in range(2):
                    i = sy.dma_start(
                        out=kfull[2 * b + e, :, tc * 4:(tc + 1) * 4, :],
                        in_=ksb[b % 2][64 * e:64 * e + 64, :].rearrange(
                            "p (c k) -> p c k", c=4)
                    ).then_inc(sems["s_kvst"], 16)
                return i
            pl.step("sync", f_kst, waits=[("s_dv", V("s_dv"))],
                    incs=[("s_kvst", 32)], attach=False)
            MS["kst_%d_%d" % (tc, b)] = V("s_kvst")

        for ts in (0, 1, 3, 2):
            pslot = psB[:, ts % 2, :]

            def f_vmm(t, tc=tc, sl=sl, ts=ts, pslot=pslot):
                for dc in range(G16):
                    t.matmul(pslot,
                             xh[sl][:, dc * 512 + ts * 128: dc * 512 + (ts + 1) * 128],
                             wvh[:, dc * 512:(dc + 1) * 512],
                             start=(dc == 0), stop=False)
                return t.matmul(
                    pslot,
                    mrowh[0:1, tc * TCW + ts * 128: tc * TCW + (ts + 1) * 128],
                    wsvn[0:1, :], start=False, stop=True)
            wts = []
            prev = ("vev_dv_%d_%d" % (tc, ts - 2)) if ts >= 2 else \
                   ("vev_dv_%d_%d" % (tc - 1, ts + 2)) if tc >= 1 else None
            if prev:
                wts.append(("s_dv", MS[prev]))
            pl.step("tensor", f_vmm, waits=wts, incs=[("s_pe", 1)])
            pe_v = V("s_pe")

            def f_vev(v, tc=tc, ts=ts, pslot=pslot):
                return v.tensor_scalar(
                    out=vst[:, :].rearrange("p (h c d) -> p h c d", h=8, d=65)[:, :, ts, 0:64],
                    in0=pslot.rearrange("p (h d) -> p h d", h=8),
                    scalar1=rstdcol[:, tc * 4 + ts: tc * 4 + ts + 1],
                    scalar2=None, op0=ALU.mult)
            wts = [("s_pe", pe_v), ("s_dv", MS["tcol_dv_%d" % tc])]
            if tc >= 1 and ts == 0:
                wts.append(("s_kvst", MS["vst_%d" % (tc - 1)]))
            pl.step("vector", f_vev, waits=wts, incs=[("s_dv", 1)])
            MS["vev_dv_%d_%d" % (tc, ts)] = V("s_dv")

        def f_vst(sy, tc=tc):
            vflat = vfull.rearrange("h p c d -> h p (c d)")
            for h in range(8):
                i = sy.dma_start(
                    out=vflat[h, :, tc * 260:(tc + 1) * 260],
                    in_=vst[:, h * 260:(h + 1) * 260]
                ).then_inc(sems["s_kvst"], 16)
            return i
        pl.step("sync", f_vst, waits=[("s_dv", V("s_dv")), ("s_po", 2)],
                incs=[("s_kvst", 16 * 8)], attach=False)
        MS["vst_%d" % tc] = V("s_kvst")
        pe_proj_done[tc] = V("s_pe")

    for tc in range(NTC + 1):
        if tc < NTC:
            emit_stats(tc)
        if tc >= 1:
            emit_proj(tc - 1)
    MS["p1_pe"] = V("s_pe")
    MS["p1_dv"] = V("s_dv")
    MS["p1_kvst"] = V("s_kvst")
    dbg_dump("vst_dbg", vst_full[:, :], [("s_dv", V("s_dv"))])
    if "v7_dbg" in dbg_out:
        pl.step("sync", lambda sy: sy.dma_start(
            out=dbg_out["v7_dbg"][:, :],
            in_=vfull[7, :, :, :].rearrange("p c d -> p (c d)")),
            waits=[("s_kvst", V("s_kvst"))], incs=[("s_st", 16)])
    dbg_dump("mr_dbg", mrowh[:, :], [("s_dv", MS["rows_dv_%d" % (NTC - 1)])])
    dbg_dump("rs_dbg", rstdrowh[:, :], [("s_ac", MS["rstd_ac_%d" % (NTC - 1)])])
    dbg_dump("rc_dbg", rstdcol[:, :], [("s_dv", MS["p1_dv"])])

    def f_qbc_prep1(v):
        v.tensor_copy(out=bch[:, :], in_=mq32[0:1, :])
        return v.tensor_tensor(bcl[:, :], mq32[0:1, :], bch[:, :], ALU.subtract)
    pl.step("vector", f_qbc_prep1, waits=[("s_ac", MS["qrstd_ac"])],
            incs=[("s_dv", 1)])

    def f_qbc1(t):
        for t2 in range(2):
            t.matmul(psA[:, t2, :], onesrh[:, :], bch[0:1, t2 * 512:(t2 + 1) * 512],
                     start=True, stop=False)
            i = t.matmul(psA[:, t2, :], onesrh[:, :], bcl[0:1, t2 * 512:(t2 + 1) * 512],
                         start=False, stop=True)
        return i
    pl.step("tensor", f_qbc1, waits=[("s_dv", V("s_dv")), ("s_dv", MS["p1_dv"])],
            incs=[("s_pe", 1)])

    def f_qbc_prep2(v):
        v.tensor_copy(out=bch[:, :], in_=vq32[0:1, :])
        return v.tensor_tensor(bcl[:, :], vq32[0:1, :], bch[:, :], ALU.subtract)
    pl.step("vector", f_qbc_prep2, waits=[("s_pe", V("s_pe"))], incs=[("s_dv", 1)])

    def f_qbc2(t):
        for t2 in range(2):
            t.matmul(psB[:, t2, :], onesrh[:, :], bch[0:1, t2 * 512:(t2 + 1) * 512],
                     start=True, stop=False)
            i = t.matmul(psB[:, t2, :], onesrh[:, :], bcl[0:1, t2 * 512:(t2 + 1) * 512],
                         start=False, stop=True)
        return i
    pl.step("tensor", f_qbc2, waits=[("s_dv", V("s_dv"))], incs=[("s_pe", 1)])

    def f_qbc_ev(v):
        v.tensor_copy(out=mbcq[:, :], in_=psA[:, :, :].rearrange("p a b -> p (a b)"))
        return v.tensor_copy(out=rbcq[:, :],
                             in_=psB[:, :, :].rearrange("p a b -> p (a b)"))
    pl.step("vector", f_qbc_ev, waits=[("s_pe", V("s_pe"))], incs=[("s_dv", 1)])
    MS["qbc_dv"] = V("s_dv")

    for j in range(4):
        def f_ln1q(v, j=j):
            for g in range(4 * j, 4 * j + 4):
                cs = g * 1024
                v.tensor_tensor(xqh[:, cs:cs + 1024], xqh[:, cs:cs + 1024],
                                mbcq[:, :], ALU.subtract)
                i = v.tensor_tensor(xqh[:, cs:cs + 1024], xqh[:, cs:cs + 1024],
                                    rbcq[:, :], ALU.mult)
            return i
        pl.step("vector", f_ln1q, waits=[("s_dv", MS["qbc_dv"])],
                incs=[("s_dv", 1)])
        MS["ln1q_dv_%d" % j] = V("s_dv")
    MS["ln1q_dv"] = V("s_dv")
    dbg_dump("mbc_dbg", mbcq[:, :], [("s_dv", MS["qbc_dv"])])
    dbg_dump("rbc_dbg", rbcq[:, :], [("s_dv", MS["qbc_dv"])])
    dbg_dump("ln1q_dbg", xqh[:, :], [("s_dv", MS["ln1q_dv"])])

    for g16 in range(G16):
        sl = g16 % 2
        sq = "s_q%d" % sl

        def f_wqld(g, g16=g16, sl=sl):
            return g.dma_start(
                out=wqh[sl][:, :].rearrange("p (g n) -> p g n", g=16),
                in_=wq[:, g16 * 128:(g16 + 1) * 128].rearrange("(g p) n -> p g n", p=128))
        wts = [(sq, V(sq))]
        if g16 >= 2:
            wts.append(("s_pe", MS["qproj_pe_%d" % (g16 - 2)]))
        else:
            wts.append(("s_kvst", MS["p1_kvst"]))
        pl.step("gpsimd", f_wqld, waits=wts, incs=[(sq, 16)])

        for j in range(4):
            def f_qproj(t, g16=g16, sl=sl, j=j):
                pm = psA if g16 % 2 == 0 else psB
                for t2 in range(2):
                    for dc in range(4 * j, 4 * j + 4):
                        i = t.matmul(pm[:, t2, :], wqh[sl][:, dc * 128:(dc + 1) * 128],
                                     xqh[:, dc * 1024 + t2 * 512: dc * 1024 + (t2 + 1) * 512],
                                     start=(dc == 0), stop=(dc == 15))
                return i
            wts = [(sq, V(sq)), ("s_dv", MS["ln1q_dv_%d" % j])]
            if j == 0 and g16 >= 2:
                wts.append(("s_dv", MS["qev_dv_%d" % (g16 - 2)]))
            pl.step("tensor", f_qproj, waits=wts, incs=[("s_pe", 1)])
        MS["qproj_pe_%d" % g16] = V("s_pe")

        def f_qev(v, g16=g16):
            pm = psA if g16 % 2 == 0 else psB
            return v.tensor_copy(out=qsb[:, g16 * 1024:(g16 + 1) * 1024],
                                 in_=pm[:, :, :].rearrange("p a b -> p (a b)"))
        pl.step("vector", f_qev, waits=[("s_pe", V("s_pe"))], incs=[("s_dv", 1)])
        MS["qev_dv_%d" % g16] = V("s_dv")
    MS["p1c_pe"] = V("s_pe")
    MS["p1c_dv"] = V("s_dv")
    dbg_dump("q_dbg", qsb[:, :], [("s_dv", MS["p1c_dv"])])

    MS["exps_prev"] = [0, 0]
    MS["oev_guard"] = [0, 0]
    MS["bcsh_guard"] = 0

    passes = []
    for h in range(HKV):
        for qt in range(2):
            for pr in range(2):
                passes.append((h, qt, pr, 32 if qt == 0 else 64))
    sev = []
    for h, qt, pr, kcmax in passes:
        for kc in range(kcmax):
            sev.append((h, qt, pr, kc, kcmax))
    NEV = len(sev)
    kb_ready_h = {}
    vb_ready_h = {}
    exp_hist = {}
    mask_hist = {}

    def emit_kbld(h):
        sl = h % 2
        skb, svb = "s_kb%d" % sl, "s_vb%d" % sl

        def f_kbld(sy, h=h, sl=sl, skb=skb, svb=svb):
            sy.dma_start(out=kb[sl][0:64, :],
                         in_=kfull[h, :, :, :].rearrange("d c k -> d (c k)")
                         ).then_inc(sems[skb], 16)
            sy.dma_start(out=kb[sl][64:128, :],
                         in_=kfull[h, :, :, :].rearrange("d c k -> d (c k)")
                         ).then_inc(sems[skb], 16)
            return sy.dma_start(out=vb[sl][:, :],
                                in_=vfull[h, :, :, :].rearrange("p c d -> p (c d)")
                                ).then_inc(sems[svb], 16)
        wts = [("s_kvst", MS["p1_kvst"]), (skb, V(skb)), (svb, V(svb))]
        if h >= 2:
            wts.append(("s_pe", MS["attn_pe_h%d" % (h - 2)]))
        else:
            wts.append(("s_pe", MS["p1c_pe"]))
        pl.step("sync", f_kbld, waits=wts,
                incs=[(skb, 32), (svb, 16)], attach=False)
        kb_ready_h[h] = V(skb)
        vb_ready_h[h] = V(svb)

    def f_okones(g):
        g.memset(oneslo[:, :], 0.0)
        g.memset(oneslo[0:1, 0:64], 1.0)
        g.memset(oneshi[:, :], 0.0)
        return g.memset(oneshi[0:1, 64:128], 1.0)
    pl.step("gpsimd", f_okones, waits=[("s_kvst", MS["p1_kvst"])],
            incs=[("s_po", 1)])
    MS["okones_po"] = V("s_po")

    emit_kbld(0)
    emit_kbld(1)

    def emit_scores(gi):
        h, qt, pr, kc, kcmax = sev[gi]
        sl = h % 2
        cs = max(0, 16 * kc - 512 * qt)
        nq = 512 - cs
        qc0 = qt * 512 + cs
        pslot = gi % 4
        sb_ = psA if gi % 2 == 0 else psB

        def f_scores(t, h=h, pr=pr, sl=sl, kc=kc, cs=cs, nq=nq, qc0=qc0, sb_=sb_):
            ch = (2 * h + pr) * 1024
            t.matmul(sb_[:, 0, cs:512],
                     kb[sl][0:64, kc * 128:(kc + 1) * 128],
                     qsb[0:64, ch + qc0: ch + qc0 + nq],
                     start=True, stop=True, tile_position=(0, 0))
            return t.matmul(sb_[:, 1, cs:512],
                            kb[sl][64:128, kc * 128:(kc + 1) * 128],
                            qsb[64:128, ch + qc0: ch + qc0 + nq],
                            start=True, stop=True, tile_position=(64, 0))
        pl.step("tensor", f_scores,
                waits=[("s_kb%d" % sl, kb_ready_h[h]),
                       ("s_ac", MS["exps_prev"][gi % 2])],
                incs=[("s_pe", 1)])
        pe_s = V("s_pe")

        def f_exp(s, pslot=pslot, cs=cs, sb_=sb_):
            return s.activation(
                out=pb[pslot][:, :].rearrange("p (j n) -> p j n", j=2)[:, :, cs:512],
                in_=sb_[:, :, cs:512], func=AF.Exp, scale=0.125)
        pl.step("scalar", f_exp, waits=[("s_pe", pe_s)], incs=[("s_ac", 1)])
        MS["exps_prev"][gi % 2] = V("s_ac")
        exp_hist[gi] = V("s_ac")

        if 16 * kc >= 512 * qt:
            def f_mask(g, pslot=pslot, cs=cs, pr=pr):
                pv = pb[pslot][:, :].rearrange("p (j n) -> p j n", j=2)
                return g.tensor_tensor(
                    pv[:, :, cs:cs + 16], pv[:, :, cs:cs + 16],
                    maskq4[:, :].rearrange("p (j n) -> p j n", j=4)[:, 2 * pr:2 * pr + 2, :],
                    ALU.mult)
            pl.step("gpsimd", f_mask, waits=[("s_ac", V("s_ac"))],
                    incs=[("s_po", 1)])
        mask_hist[gi] = V("s_po")

    def emit_pv(gi):
        h, qt, pr, kc, kcmax = sev[gi]
        sl = h % 2
        pcs = max(0, 16 * kc - 512 * qt)
        ppslot = gi % 4
        first = (kc == 0)
        last = (kc == kcmax - 1)
        obank = psC if pr == 0 else psD

        ebank_pv = psE if pr == 0 else psF

        def f_pv(t, sl=sl, pcs=pcs, ppslot=ppslot, first=first,
                 last=last, kcm=kc, obank=obank, ebank_pv=ebank_pv):
            pv = pb[ppslot][:, :].rearrange("p (j n) -> p j n", j=2)
            vw = vb[sl][:, kcm * 65:(kcm + 1) * 65]
            t.matmul(obank[0:65, pcs:512], vw, pv[:, 0, pcs:512],
                     start=first, stop=last, tile_position=(0, 0),
                     skip_group_check=True)
            return t.matmul(ebank_pv[0:65, pcs:512], vw, pv[:, 1, pcs:512],
                            start=first, stop=last, tile_position=(0, 0),
                            skip_group_check=True)
        wts = [("s_ac", exp_hist[gi]), ("s_vb%d" % sl, vb_ready_h[h]),
               ("s_po", mask_hist[gi])]
        if first:
            wts.append(("s_dv", MS["oev_guard"][pr]))
        pl.step("tensor", f_pv, waits=wts, incs=[("s_pe", 1)])

        if last:
            ebank = psE if pr == 0 else psF

            def f_den(v, h=h, qt=qt, pr=pr, obank=obank, ebank=ebank):
                c0 = (2 * h + pr) * 1024 + qt * 512
                v.tensor_copy(out=osb[0:64, c0:c0 + 512], in_=obank[0:64, :])
                v.tensor_copy(out=oshift[:, :], in_=ebank[0:64, :])
                v.tensor_copy(out=drow[0:1, 0:512], in_=obank[64:65, :])
                return v.tensor_copy(out=drow[0:1, 512:1024], in_=ebank[64:65, :])
            pl.step("vector", f_den, waits=[("s_pe", V("s_pe"))], incs=[("s_dv", 1)])
            den_dv = V("s_dv")

            def emit_bcsh(h=h, qt=qt, pr=pr, obank=obank, ebank=ebank,
                          den_dv=den_dv):
                def f_bcsh(t, obank=obank, ebank=ebank):
                    t.matmul(obank[:, :], oneslo[:, :], drow[0:1, 0:512],
                             start=True, stop=False, skip_group_check=True)
                    t.matmul(obank[:, :], oneshi[:, :], drow[0:1, 512:1024],
                             start=False, stop=True, skip_group_check=True)
                    return t.matmul(ebank[:, :], shfh[:, :], oshift[:, :],
                                    start=True, stop=True, skip_group_check=True)
                pl.step("tensor", f_bcsh,
                        waits=[("s_dv", den_dv), ("s_po", MS["okones_po"])],
                        incs=[("s_pe", 1)])

            def emit_n1(h=h, qt=qt, pr=pr, obank=obank, ebank=ebank):
                def f_n1(v, h=h, qt=qt, pr=pr, obank=obank, ebank=ebank):
                    c0 = (2 * h + pr) * 1024 + qt * 512
                    v.reciprocal(rdb[:, :], obank[:, :])
                    v.tensor_tensor(osb[0:64, c0:c0 + 512],
                                    osb[0:64, c0:c0 + 512],
                                    rdb[0:64, :], ALU.mult)
                    return v.tensor_tensor(osb[64:128, c0:c0 + 512],
                                           ebank[64:128, :],
                                           rdb[64:128, :], ALU.mult)
                pl.step("vector", f_n1, waits=[("s_pe", V("s_pe"))],
                        incs=[("s_dv", 1)])
                MS["oev_guard"][pr] = V("s_dv")

            pend.setdefault(cur_gi + 4, []).append(emit_bcsh)
            pend.setdefault(cur_gi + 8, []).append(emit_n1)

            if qt == 1 and pr == 1:
                MS["attn_pe_h%d" % h] = V("s_pe")
                if h + 2 < HKV:
                    emit_kbld(h + 2)

    pend = {}
    cur_gi = 0
    for gi in range(NEV + 12):
        cur_gi = gi
        if gi < NEV:
            emit_scores(gi)
        if 3 <= gi < NEV + 3:
            emit_pv(gi - 3)
        for fn in pend.pop(gi, []):
            fn()
    MS["attn_pe"] = V("s_pe")
    MS["attn_dv"] = V("s_dv")
    dbg_dump("o_dbg", osb[:, :], [("s_dv", MS["attn_dv"])])

    def f_xbld(sy):
        return sy.dma_start(out=xb[:, :].rearrange("p (g t) -> p g t", g=16),
                            in_=xq.rearrange("(g p) t -> p g t", p=128))
    pl.step("sync", f_xbld, waits=[("s_pe", MS["attn_pe"]), ("s_dv", MS["attn_dv"])],
            incs=[("s_xb", 16)])

    def emit_wo(g16):
        sl = g16 % 2
        sw = "s_wo%d" % sl

        def f_wold(g, g16=g16, sl=sl, sw=sw):
            return g.dma_start(
                out=woh[sl][:, :].rearrange("p (g n) -> p g n", g=16),
                in_=wo[:, g16 * 128:(g16 + 1) * 128].rearrange("(g p) n -> p g n", p=128))
        wts = [(sw, V(sw)), ("s_dv", MS["ln1q_dv"])]
        if g16 >= 2:
            wts.append(("s_pe", MS["wo_pe_%d" % (g16 - 2)]))
        pl.step("gpsimd", f_wold, waits=wts, incs=[(sw, 16)])

        def f_womm(t, g16=g16, sl=sl):
            pm = psA if g16 % 2 == 0 else psB
            for t2 in range(2):
                for dc in range(G16):
                    i = t.matmul(pm[:, t2, :], woh[sl][:, dc * 128:(dc + 1) * 128],
                                 osb[:, dc * 1024 + t2 * 512: dc * 1024 + (t2 + 1) * 512],
                                 start=(dc == 0), stop=(dc == 15))
            return i
        wts = [(sw, V(sw)), ("s_dv", MS["attn_dv"])]
        if g16 >= 2:
            wts.append(("s_dv", MS["wo_dv_%d" % (g16 - 2)]))
        pl.step("tensor", f_womm, waits=wts, incs=[("s_pe", 1)])
        MS["wo_pe_%d" % g16] = V("s_pe")

        def f_woev(v, g16=g16):
            pm = psA if g16 % 2 == 0 else psB
            return v.tensor_tensor(xb[:, g16 * 1024:(g16 + 1) * 1024],
                                   pm[:, :, :].rearrange("p a b -> p (a b)"),
                                   xb[:, g16 * 1024:(g16 + 1) * 1024], ALU.add)
        pl.step("vector", f_woev, waits=[("s_pe", V("s_pe")), ("s_xb", 16)],
                incs=[("s_dv", 1)])
        MS["wo_dv_%d" % g16] = V("s_dv")

    def emit_rstat(g16):
        def f_rh(g, g16=g16):
            g.tensor_copy(out=rh3[:, :], in_=xb[:, g16 * 1024:(g16 + 1) * 1024])
            return g.tensor_mul(rsq3[:, :], rh3[:, :], rh3[:, :])
        wts = [("s_dv", MS["wo_dv_%d" % g16])]
        if g16 >= 1:
            wts.append(("s_pe", MS["rstat_pe_%d" % (g16 - 1)]))
        pl.step("gpsimd", f_rh, waits=wts, incs=[("s_po", 1)])

        wts = [("s_po", V("s_po"))]
        if g16 == 0:
            wts.append(("s_dv", MS["attn_dv"]))

        def f_rstat(t, g16=g16):
            for t2 in range(2):
                pm = psE if t2 == 0 else psF
                t.matmul(pm[0:64, :], ones64[:, :], rh3[:, t2 * 512:(t2 + 1) * 512],
                         start=(g16 == 0), stop=(g16 == 15))
            for t2 in range(2):
                pm = psC if t2 == 0 else psD
                i = t.matmul(pm[0:64, :], ones64[:, :], rsq3[:, t2 * 512:(t2 + 1) * 512],
                             start=(g16 == 0), stop=(g16 == 15))
            return i
        pl.step("tensor", f_rstat, waits=wts, incs=[("s_pe", 1)])
        MS["rstat_pe_%d" % g16] = V("s_pe")

    for g16 in range(G16 + 1):
        if g16 < G16:
            emit_wo(g16)
        if g16 >= 1:
            emit_rstat(g16 - 1)

    def f_r3rows(v):
        for t2 in range(2):
            pm = psE if t2 == 0 else psF
            pv2 = psC if t2 == 0 else psD
            v.tensor_scalar_mul(m3[0:1, t2 * 512:(t2 + 1) * 512], pm[0:1, :], 1.0 / D)
            v.tensor_scalar_mul(v3[0:1, t2 * 512:(t2 + 1) * 512], pv2[0:1, :], 1.0 / D)
            v.tensor_mul(pm[0:1, :], m3[0:1, t2 * 512:(t2 + 1) * 512],
                         m3[0:1, t2 * 512:(t2 + 1) * 512])
            i = v.tensor_tensor(v3[0:1, t2 * 512:(t2 + 1) * 512],
                                v3[0:1, t2 * 512:(t2 + 1) * 512], pm[0:1, :],
                                ALU.subtract)
        return v.tensor_scalar_add(v3[0:1, :], v3[0:1, :], EPS)
    pl.step("vector", f_r3rows, waits=[("s_pe", V("s_pe"))], incs=[("s_dv", 1)])

    def f_r3std(s):
        s.activation(out=sd3[0:1, :], in_=v3[0:1, :], func=AF.Ln)
        return s.activation(out=v3[0:1, :], in_=sd3[0:1, :], func=AF.Exp, scale=0.5)
    pl.step("scalar", f_r3std, waits=[("s_dv", V("s_dv"))], incs=[("s_ac", 1)])

    def f_hilo(v):
        v.scalar_tensor_tensor(hirow[0:1, :], v3[0:1, :], 0.33, m3[0:1, :],
                               ALU.mult, ALU.add)
        return v.scalar_tensor_tensor(lorow[0:1, :], v3[0:1, :], -0.33, m3[0:1, :],
                                      ALU.mult, ALU.add)
    pl.step("vector", f_hilo, waits=[("s_ac", V("s_ac"))], incs=[("s_dv", 1)])

    def f_hibc_prep(v):
        v.tensor_copy(out=bch[:, :], in_=hirow[0:1, :])
        return v.tensor_tensor(bcl[:, :], hirow[0:1, :], bch[:, :], ALU.subtract)
    pl.step("vector", f_hibc_prep, waits=[("s_dv", V("s_dv"))], incs=[("s_dv", 1)])

    def f_hibc(t):
        for t2 in range(2):
            pm = psE if t2 == 0 else psF
            t.matmul(pm[:, :], onesrh[:, :], bch[0:1, t2 * 512:(t2 + 1) * 512],
                     start=True, stop=False)
            i = t.matmul(pm[:, :], onesrh[:, :], bcl[0:1, t2 * 512:(t2 + 1) * 512],
                         start=False, stop=True)
        return i
    pl.step("tensor", f_hibc, waits=[("s_dv", V("s_dv"))], incs=[("s_pe", 1)])

    def f_lobc_prep(v):
        v.tensor_copy(out=bch[:, :], in_=lorow[0:1, :])
        return v.tensor_tensor(bcl[:, :], lorow[0:1, :], bch[:, :], ALU.subtract)
    pl.step("vector", f_lobc_prep, waits=[("s_pe", V("s_pe"))], incs=[("s_dv", 1)])

    def f_lobc(t):
        for t2 in range(2):
            pm = psC if t2 == 0 else psD
            t.matmul(pm[:, :], onesrh[:, :], bch[0:1, t2 * 512:(t2 + 1) * 512],
                     start=True, stop=False)
            i = t.matmul(pm[:, :], onesrh[:, :], bcl[0:1, t2 * 512:(t2 + 1) * 512],
                         start=False, stop=True)
        return i
    pl.step("tensor", f_lobc, waits=[("s_dv", V("s_dv"))], incs=[("s_pe", 1)])

    def f_hilobc_ev(v):
        v.tensor_copy(out=hibc[:, 0:512], in_=psE[:, :])
        v.tensor_copy(out=hibc[:, 512:1024], in_=psF[:, :])
        v.tensor_copy(out=lobc[:, 0:512], in_=psC[:, :])
        return v.tensor_copy(out=lobc[:, 512:1024], in_=psD[:, :])
    pl.step("vector", f_hilobc_ev, waits=[("s_pe", V("s_pe"))], incs=[("s_dv", 1)])
    MS["hilo_dv"] = V("s_dv")

    for g16 in range(G16):
        def f_g1tq(v, g16=g16):
            v.tensor_tensor(g2b[:, :], xb[:, g16 * 1024:(g16 + 1) * 1024],
                            lobc[:, :], ALU.is_lt)
            v.tensor_tensor(g1b[:, :], xb[:, g16 * 1024:(g16 + 1) * 1024],
                            hibc[:, :], ALU.is_gt)
            return v.tensor_tensor(tqsb[:, g16 * 1024:(g16 + 1) * 1024],
                                   g1b[:, :], g2b[:, :], ALU.subtract)
        pl.step("vector", f_g1tq, waits=[("s_dv", MS["hilo_dv"])],
                incs=[("s_dv", 1)])
        MS["tq_dv_%d" % g16] = V("s_dv")
    MS["tq_dv"] = V("s_dv")

    def emit_ffn_front(g16):
        sl = g16 % 2
        sw = "s_wf%d" % sl

        def f_wfld(g, g16=g16, sl=sl, sw=sw):
            g.dma_start(
                out=wfh[sl][:, :].rearrange("p (g n) -> p g n", g=16),
                in_=wft[:, g16 * 128:(g16 + 1) * 128].rearrange("(g p) n -> p g n", p=128)
            ).then_inc(sems[sw], 16)
            return g.dma_start(
                out=rrh[sl][:, :].rearrange("p (g n) -> p g n", g=16),
                in_=rrt[:, g16 * 128:(g16 + 1) * 128].rearrange("(g p) n -> p g n", p=128)
            ).then_inc(sems[sw], 16)
        wts = [(sw, V(sw))]
        if g16 >= 2:
            wts.append(("s_pe", MS["ffn_pe_%d" % (g16 - 2)]))
        pl.step("gpsimd", f_wfld, waits=wts, incs=[(sw, 32)], attach=False)

        def f_tern(v, sl=sl):
            v.tensor_single_scalar(tt1[:, :], wfh[sl][:, :], 0.2, ALU.is_gt)
            v.tensor_single_scalar(tt2[:, :], wfh[sl][:, :], -0.2, ALU.is_lt)
            v.tensor_tensor(tt1[:, :], tt1[:, :], tt2[:, :], ALU.subtract)
            return v.scalar_tensor_tensor(wfh[sl][:, :], tt1[:, :], 0.6,
                                          rrh[sl][:, :], ALU.mult, ALU.add)
        pl.step("vector", f_tern, waits=[(sw, V(sw))], incs=[("s_dv", 1)])
        MS["tern_dv_%d" % g16] = V("s_dv")

        def f_ffnmm(t, g16=g16, sl=sl):
            pm = psA if g16 % 2 == 0 else psB
            for t2 in range(2):
                for dc in range(G16):
                    i = t.matmul(pm[:, t2, :], wfh[sl][:, dc * 128:(dc + 1) * 128],
                                 tqsb[:, dc * 1024 + t2 * 512: dc * 1024 + (t2 + 1) * 512],
                                 start=(dc == 0), stop=(dc == 15))
            return i
        wts = [("s_dv", MS["tern_dv_%d" % g16]), ("s_dv", MS["tq_dv"])]
        if g16 >= 2:
            wts.append(("s_dv", MS["ffnev_dv_%d" % (g16 - 2)]))
        pl.step("tensor", f_ffnmm, waits=wts, incs=[("s_pe", 1)])
        MS["ffn_pe_%d" % g16] = V("s_pe")

    def emit_ffn_back(g16):
        sl = g16 % 2

        def f_ffnev(v, g16=g16, sl=sl):
            pm = psA if g16 % 2 == 0 else psB
            return v.scalar_tensor_tensor(outs[sl][:, :],
                                          pm[:, :, :].rearrange("p a b -> p (a b)"),
                                          0.5, xb[:, g16 * 1024:(g16 + 1) * 1024],
                                          ALU.mult, ALU.add)
        wts = [("s_pe", MS["ffn_pe_%d" % g16])]
        if g16 >= 2:
            wts.append(("s_st", MS["ost_%d" % (g16 - 2)]))
        pl.step("vector", f_ffnev, waits=wts, incs=[("s_dv", 1)])
        MS["ffnev_dv_%d" % g16] = V("s_dv")

        def f_ost(sy, g16=g16, sl=sl):
            return sy.dma_start(out=out[g16 * 128:(g16 + 1) * 128, :],
                                in_=outs[sl][:, :])
        pl.step("sync", f_ost, waits=[("s_dv", V("s_dv"))], incs=[("s_st", 16)])
        MS["ost_%d" % g16] = V("s_st")

    for g16 in range(G16 + 1):
        if g16 < G16:
            emit_ffn_front(g16)
        if g16 >= 1:
            emit_ffn_back(g16 - 1)

    if "k_dbg" in dbg_out:
        pl.step("sync", lambda sy: sy.dma_start(
            out=dbg_out["k_dbg"][:, :],
            in_=kfull.rearrange("h d c k -> (h d) (c k)")),
            waits=[("s_kvst", MS["p1_kvst"])], incs=[("s_st", 16)])
    if "v_dbg" in dbg_out:
        pl.step("sync", lambda sy: sy.dma_start(
            out=dbg_out["v_dbg"][:, :],
            in_=vfull.rearrange("h p c d -> (h p) (c d)")),
            waits=[("s_kvst", MS["p1_kvst"])], incs=[("s_st", 16)])
    if "tq_dbg" in dbg_out:
        pl.step("sync", lambda sy: sy.dma_start(
            out=dbg_out["tq_dbg"][:, :], in_=tqsb[:, :]),
            waits=[("s_dv", MS["tq_dv"])], incs=[("s_st", 16)])
    st_total = V("s_st")
    pl.step("sync", lambda sy: sy.nop(), waits=[("s_st", st_total)])

    for (_e, _f, _w, _i, _a) in pl.steps:
        for (sname, val) in _w:
            assert val <= pl.cnt.get(sname, 0), (
                "unsatisfiable wait %s >= %d (max %d)" % (sname, val, pl.cnt.get(sname, 0)))

    with nc.Block() as block:
        def make(engname):
            def body(eng):
                last_wait = {}
                for (e, fn, waits, incs, attach) in pl.steps:
                    if e != engname:
                        continue
                    for (sname, val) in waits:
                        if last_wait.get(sname, -1) < val:
                            eng.wait_ge(sems[sname], val)
                            last_wait[sname] = val
                    inst = fn(eng)
                    if attach:
                        for (sname, amt) in incs:
                            inst = inst.then_inc(sems[sname], amt)
            return body
        block.sync(make("sync"))
        block.tensor(make("tensor"))
        block.vector(make("vector"))
        block.scalar(make("scalar"))
        block.gpsimd(make("gpsimd"))

    for ctx in reversed(sem_ctxs):
        ctx.__exit__(None, None, None)

    return nc


def make_inputs(L0_attn_x, L0_attn_Wq, L0_attn_Wk, L0_attn_Wv, L0_attn_Wo,
                L0_ffn_weight, L0_ffn_rram_variation):
    xT = np.ascontiguousarray(np.asarray(L0_attn_x, np.float32)[0].T)
    wftT = np.ascontiguousarray(np.asarray(L0_ffn_weight, np.float32).T)
    rrtT = np.ascontiguousarray(np.asarray(L0_ffn_rram_variation, np.float32).T)
    in_maps = []
    for c in range(NCORE):
        kk = np.arange(128)[:, None]
        jj = np.arange(16)[None, :]
        mqc = (kk <= 8 * jj + c).astype(np.float32)
        in_maps.append({
            "xq": np.ascontiguousarray(xT[:, c::8]),
            "xf": xT,
            "wq": np.asarray(L0_attn_Wq, np.float32),
            "wk": np.asarray(L0_attn_Wk, np.float32),
            "wv": np.asarray(L0_attn_Wv, np.float32),
            "wo": np.asarray(L0_attn_Wo, np.float32),
            "wft": wftT,
            "rrt": rrtT,
            "mq": mqc,
            "shf": np.concatenate([np.zeros((64, 64), np.float32),
                                   np.eye(64, dtype=np.float32)], axis=1),
        })
    return in_maps


def kernel(L0_attn_x, L0_attn_Wq, L0_attn_Wk, L0_attn_Wv, L0_attn_Wo,
           L0_ffn_weight, L0_ffn_rram_variation):
    in_maps = make_inputs(L0_attn_x, L0_attn_Wq, L0_attn_Wk, L0_attn_Wv,
                          L0_attn_Wo, L0_ffn_weight, L0_ffn_rram_variation)
    nc = build()
    res = run_bass_kernel_spmd(nc, in_maps, core_ids=list(range(NCORE)))
    y = np.empty((1, S, D), np.float32)
    for c in range(NCORE):
        y[0, c::8, :] = np.asarray(res.results[c]["out"], np.float32).T
    return y
